# version 15
# baseline (speedup 1.0000x reference)
import contextlib

import numpy as np

import concourse.bass as bass
import concourse.mybir as mybir
from concourse.bass import IndirectOffsetOnAxis
from concourse.bass_utils import run_bass_kernel_spmd
from concourse.library_config import mlp as _mlp_lib

F32 = mybir.dt.float32
BF16 = mybir.dt.bfloat16
I16 = mybir.dt.int16
I32 = mybir.dt.int32
ACT = mybir.ActivationFunctionType
AL = mybir.AluOpType
AXX = mybir.AxisListType.X

NCORES = 8
TCORE = 2048
NT = 16
D = 1024
DC = 8
NPAIR = 8
DHID = 4096
MHID = 32
NEXP = 8
CAP = 2560
ESTR = 2688
ILC = 168
RCH = 512
NCH = 5
GQ = 128
NGQ = TCORE // GQ
EPS_LN = 1e-5
STAB = 1e-6

_STAGE = "full"


def build_nc():
    nc = bass.Bass("TRN2", target_bir_lowering=False)

    xin = nc.dram_tensor("xin", [TCORE, D], F32, kind="ExternalInput")
    wqkv = nc.dram_tensor("wqkv", [3, D, D], F32, kind="ExternalInput")
    rfin = nc.dram_tensor("rfin", [16, 64, 64], F32, kind="ExternalInput")
    wout = nc.dram_tensor("wout", [D, D], F32, kind="ExternalInput")
    bout = nc.dram_tensor("bout", [D], F32, kind="ExternalInput")
    rww = nc.dram_tensor("rww", [128, DC * NEXP], F32, kind="ExternalInput")
    rwb = nc.dram_tensor("rwb", [NEXP], F32, kind="ExternalInput")
    w1e = nc.dram_tensor("w1e", [MHID * 128, DC * 128], F32, kind="ExternalInput")
    b1e = nc.dram_tensor("b1e", [128, MHID], F32, kind="ExternalInput")
    w2e = nc.dram_tensor("w2e", [DHID, D], F32, kind="ExternalInput")
    b2e = nc.dram_tensor("b2e", [D], F32, kind="ExternalInput")
    tri = nc.dram_tensor("tri", [128, 128], F32, kind="ExternalInput")
    selm = nc.dram_tensor("selm", [128, 16], F32, kind="ExternalInput")
    iota8 = nc.dram_tensor("iota8", [128, NEXP], F32, kind="ExternalInput")
    tokid = nc.dram_tensor("tokid", [128, NT], F32, kind="ExternalInput")
    onesr = nc.dram_tensor("onesr", [1, 128], F32, kind="ExternalInput")
    onec = nc.dram_tensor("onec", [128, 1], F32, kind="ExternalInput")
    id8 = nc.dram_tensor("id8", [8, 8], F32, kind="ExternalInput")
    tsel = nc.dram_tensor("tsel", [16, 2048], F32, kind="ExternalInput")
    idb = nc.dram_tensor("idb", [128, 128], BF16, kind="ExternalInput")

    out = nc.dram_tensor("out", [TCORE, D], F32, kind="ExternalOutput")

    x2d = nc.dram_tensor("x2d", [TCORE, D], F32)
    hb = nc.dram_tensor("hb", [TCORE, D], BF16)
    hall = nc.dram_tensor("hall", [NCORES * TCORE, D], BF16, addr_space="Shared")
    kvb_i = nc.dram_tensor("kvb_i", [128, 520], F32)
    kvb_o = nc.dram_tensor("kvb_o", [128, 520], F32)
    cnt_i = nc.dram_tensor("cnt_i", [16, NEXP], F32)
    cnt_o = nc.dram_tensor("cnt_o", [128, NEXP], F32, addr_space="Shared")
    tab_i = nc.dram_tensor("tab_i", [NEXP * ESTR, 1], F32)
    tab_o = nc.dram_tensor("tab_o", [ESTR, 1], F32)
    yd1 = nc.dram_tensor("yd1", [128 * NT], F32)
    yd2 = nc.dram_tensor("yd2", [128 * NT], F32)
    w2bf = nc.dram_tensor("w2bf", [DHID, D], BF16)
    w1bf = nc.dram_tensor("w1bf", [MHID * 128, DC * 128], BF16)
    eout_b = nc.dram_tensor("eout_b", [CAP, D], BF16)
    eout_all = nc.dram_tensor("eout_all", [NEXP * CAP, D], BF16, addr_space="Shared")

    ctx = contextlib.ExitStack()
    with ctx:
        AB1 = ctx.enter_context(nc.sbuf_tensor("AB1", [128, 40960], BF16))
        AB2 = ctx.enter_context(nc.sbuf_tensor("AB2", [128, 32768], BF16))
        AFM = ctx.enter_context(nc.sbuf_tensor("AFM", [128, 10256], F32))
        XGS = ctx.enter_context(nc.sbuf_tensor("XGS", [128, 8192], BF16))

        def view(t, off, shape):
            n = int(np.prod(shape))
            ap = t[:, off:off + n]
            if len(shape) == 1:
                return ap
            if len(shape) == 2:
                return ap.rearrange("p (a b) -> p a b", a=shape[0], b=shape[1])
            return ap.rearrange("p (a b c) -> p a b c", a=shape[0], b=shape[1], c=shape[2])

        FMA = view(AB1, 0, [DC, 2048])
        H1S = view(AB1, 0, [MHID, RCH])
        EOB = view(AB1, 16384, [2, 1024])
        G1B = view(AB1, 0, [NT, 1024])
        G2B = view(AB2, 0, [NT, 1024])
        TMA = view(AB1, 16384, [DC, NT, 128])
        WRF1 = view(AB1, 32768, [DC, 1024])
        WOS = WRF1
        PHQ = view(AB2, 0, [NPAIR, 2048])
        HLOT = view(AB2, 0, [DC, NT, 128])
        WRF2 = view(AB2, 16384, [DC, 1024])
        WVS = view(AB2, 24576, [DC, 1024])
        HLOF = view(AB2, 16384, [DC, 2048])
        WSLAB = view(XGS, 0, [2, 2, 1024])
        WTSLB = view(XGS, 4096, [2, 2, 1024])
        PKS = view(XGS, 0, [2, 1024])
        VAS = view(XGS, 2048, [2, 1040])
        W2C = view(XGS, 0, [4, 1024])
        XG = view(XGS, 0, [8, RCH])
        XTM = view(XGS, 4096, [4, 1024])

        XSTR = view(AFM, 0, [2, 1024])
        XNEW = view(AFM, 2048, [2, 1024])
        BCB = view(AFM, 4096, [1, 1024])
        KVCAT = view(AFM, 5120, [1, 520])
        KVAR = view(AFM, 5640, [1, 520])
        LGT = view(AFM, 6160, [1, 2048])
        KVACC = view(AFM, 6160, [8, 130])
        G1Q = view(AFM, 6160, [2, 1, 1024])
        G2Q = view(AFM, 8208, [2, 1, 1024])

        TSEL = AFM[0:16, 0:2048]
        BROW = AFM[0:1, 5120:6144]
        KVAUG = ctx.enter_context(nc.sbuf_tensor("KVAUG", [128, NPAIR, 130], BF16))
        W1S = view(AB2, 16384, [4, 8, 128])
        W2S = view(AB2, 24576, [4, 1024])
        RFBD = view(AB2, 0, [NPAIR, 128])
        STATS = ctx.enter_context(nc.sbuf_tensor("STATS", [128, 12], F32))
        MRS = ctx.enter_context(nc.sbuf_tensor("MRS", [128, NT, 4], F32))
        LTM = view(AFM, 2728, [NT, 8])
        MSK = ctx.enter_context(nc.sbuf_tensor("MSK", [128, NT, 8], F32))
        GTS = ctx.enter_context(nc.sbuf_tensor("GTS", [128, NT, 8], F32))
        POS = ctx.enter_context(nc.sbuf_tensor("POS", [128, NT, 8], F32))
        WGT = view(AFM, 2856, [NT, 8])
        E8T = view(AFM, 2472, [NT, 8])
        M12 = ctx.enter_context(nc.sbuf_tensor("M12", [128, NT, 2], F32))
        RTS = ctx.enter_context(nc.sbuf_tensor("RTS", [128, NT, 3], F32))
        SC8 = ctx.enter_context(nc.sbuf_tensor("SC8", [128, 8], F32))
        SC8B = ctx.enter_context(nc.sbuf_tensor("SC8B", [128, 8], F32))
        SCC = ctx.enter_context(nc.sbuf_tensor("SCC", [128, 16], F32))
        EPW = ctx.enter_context(nc.sbuf_tensor("EPW", [128, 4, NT], F32))
        WCOL = ctx.enter_context(nc.sbuf_tensor("WCOL", [128, 2, NT], F32))
        YOF = ctx.enter_context(nc.sbuf_tensor("YOF", [128, 2, NT], F32))
        OI1 = ctx.enter_context(nc.sbuf_tensor("OI1", [128, NT], I32))
        OIS = ctx.enter_context(nc.sbuf_tensor("OIS", [128, 4], I32))
        OI2 = ctx.enter_context(nc.sbuf_tensor("OI2", [128, NT], I32))
        TOKF = ctx.enter_context(nc.sbuf_tensor("TOKF", [128, NT], F32))
        CNTT = ctx.enter_context(nc.sbuf_tensor("CNTT", [8, 16], F32))
        CNTS = ctx.enter_context(nc.sbuf_tensor("CNTS", [16, 8], F32))
        CAG = ctx.enter_context(nc.sbuf_tensor("CAG", [128, 8], F32))
        GBS = ctx.enter_context(nc.sbuf_tensor("GBS", [128, 8], F32))
        MBS = ctx.enter_context(nc.sbuf_tensor("MBS", [16, 8], F32))
        TRIS = view(AFM, 2984, [1, 128]).rearrange("p a b -> p (a b)")
        SELS = ctx.enter_context(nc.sbuf_tensor("SELS", [128, 16], F32))
        IOTS = ctx.enter_context(nc.sbuf_tensor("IOTS", [128, 8], F32))
        ONER = ctx.enter_context(nc.sbuf_tensor("ONER", [1, 128], F32))
        ONEC = ctx.enter_context(nc.sbuf_tensor("ONEC", [128, 1], F32))
        ID8S = ctx.enter_context(nc.sbuf_tensor("ID8S", [8, 8], F32))
        RWBS = ctx.enter_context(nc.sbuf_tensor("RWBS", [8, 1], F32))
        RWF = ctx.enter_context(nc.sbuf_tensor("RWF", [128, 8, 8], F32))
        IDB = ctx.enter_context(nc.sbuf_tensor("IDB", [128, 128], BF16))
        RWHI = ctx.enter_context(nc.sbuf_tensor("RWHI", [128, 8, 8], BF16))
        RWLO = ctx.enter_context(nc.sbuf_tensor("RWLO", [128, 8, 8], BF16))
        B1S = ctx.enter_context(nc.sbuf_tensor("B1S", [128, MHID], F32))
        XI16 = ctx.enter_context(nc.sbuf_tensor("XI16", [128, ILC], I16))
        XIF = AFM[0:16, 2304:2472]
        YI16A = ctx.enter_context(nc.sbuf_tensor("YI16A", [128, 128], I16))
        YI16B = ctx.enter_context(nc.sbuf_tensor("YI16B", [128, 128], I16))
        YIFA = AFM[0:16, 2048:2176]
        YIFB = AFM[0:16, 2176:2304]
        ZTAB = view(XGS, 4096, [1, ILC])

        psA = ctx.enter_context(nc.psum_tensor("psA", [128, 2048], F32))
        psPK = ctx.enter_context(nc.psum_tensor("psPK", [128, 1024], F32))
        psV = ctx.enter_context(nc.psum_tensor("psV", [128, 1024], F32))

        def mkq(name):
            return {"h": ctx.enter_context(nc.semaphore(name)), "n": 0}

        qXL = [mkq("qXL0"), mkq("qXL1")]
        qSP = [mkq("qSP0"), mkq("qSP1")]
        qST = [mkq("qST0"), mkq("qST1")]
        qPC = [mkq("qPC0"), mkq("qPC1")]
        qMS = mkq("qMS")
        qW2 = [mkq("qW20"), mkq("qW21")]
        qTR = [mkq("qTR0"), mkq("qTR1")]
        qWQ = [mkq("qWQ0"), mkq("qWQ1")]
        qGA = [mkq("qGA0"), mkq("qGA1")]
        qGM = mkq("qGM")
        qSC = mkq("qSC")
        sCC = {"h": ctx.enter_context(nc.semaphore("sCC")), "n": 0}
        mT = {"h": ctx.enter_context(nc.semaphore("mT")), "n": 0}
        mV = {"h": ctx.enter_context(nc.semaphore("mV")), "n": 0}
        mS = {"h": ctx.enter_context(nc.semaphore("mS")), "n": 0}
        mG = {"h": ctx.enter_context(nc.semaphore("mG")), "n": 0}

        def dma(eng, q, out_, in_, prewait=True, **kw):
            if prewait and q["n"] > 0:
                eng.wait_ge(q["h"], q["n"])
            eng.dma_start(out=out_, in_=in_, **kw).then_inc(q["h"], 16)
            q["n"] += 16
            return q["n"]

        def bump(sem, inst, k=1):
            inst.then_inc(sem["h"], k)
            sem["n"] += k
            return sem["n"]

        with nc.Block() as blk:
            @blk.sync
            def _(sy):
                dma(sy, qMS, SELS[:, :], selm[:, :])
                dma(sy, qMS, IOTS[:, :], iota8[:, :])
                dma(sy, qMS, TOKF[:, :], tokid[:, :])
                dma(sy, qMS, ONER[:, :], onesr[:, :])
                dma(sy, qMS, ONEC[:, :], onec[:, :])
                dma(sy, qMS, ID8S[:, :], id8[:, :])
                dma(sy, qMS, IDB[:, :], idb[:, :])
                dma(sy, qMS, RWBS[:, :], rwb.ap().rearrange("(e x) -> e x", x=1))
                dma(sy, qMS, RWF[:, :, :], rww.ap().rearrange("q (kc e) -> q kc e", kc=DC, e=NEXP))
                dma(sy, qMS, B1S[:, :], b1e.ap())
        consts_ld = qMS["n"]

        sA0_v0 = mV["n"]
        sA0_s0 = mS["n"]
        with nc.Block() as blk:
            xac = {}

            @blk.sync
            def _(sy):
                for t in range(NT):
                    if t >= 2:
                        sy.wait_ge(mS["h"], sA0_s0 + (t - 1))
                    xac[t] = dma(sy, qXL[t % 2], XSTR[:, t % 2, :], xin[128 * t:128 * (t + 1), :])

            @blk.vector
            def _(v):
                for t in range(NT):
                    v.wait_ge(qXL[t % 2]["h"], xac[t])
                    xt = XSTR[:, t % 2, :]
                    v.bn_stats(STATS[:, 0:6], xt[:, 0:512])
                    v.bn_stats(STATS[:, 6:12], xt[:, 512:1024])
                    v.drain()
                    v.bn_aggr(MRS[:, t, 0:2], STATS[:, :])
                    v.drain()
                    v.tensor_scalar(MRS[:, t, 2:3], MRS[:, t, 1:2], 1.0, EPS_LN, AL.mult, AL.add)
                    v.drain()
                    v.reciprocal(MRS[:, t, 3:4], MRS[:, t, 2:3])
                    v.drain()
                    bump(mV, v.drain())

            @blk.scalar
            def _(sc):
                for t in range(NT):
                    sc.wait_ge(mV["h"], sA0_v0 + t + 1)
                    sc.activation(MRS[:, t, 2:3], MRS[:, t, 3:4], ACT.Sqrt)
                    sc.drain()
                    sc.mul(MRS[:, t, 3:4], MRS[:, t, 0:1], MRS[:, t, 2:3])
                    sc.drain()
                    sc.mul(MRS[:, t, 3:4], MRS[:, t, 3:4], -1.0)
                    sc.drain()
                    bump(mS, sc.activation(TMA[:, :, t, :], XSTR[:, t % 2, :], ACT.Identity,
                                           bias=MRS[:, t, 3:4], scale=MRS[:, t, 2:3]))
        ln1_S = mS["n"]

        sA1_t0 = mT["n"]
        sA1_v0 = mV["n"]
        a1T = {}
        a1V = {}
        for _p in range(NPAIR):
            for _qk in range(2):
                a1T[("st", _p, _qk)] = sA1_t0 + 4 * _p + _qk + 1
                a1V[("st", _p, _qk)] = sA1_v0 + 4 * _p + _qk + 1
                a1T[("wrf", _p, _qk)] = sA1_t0 + 4 * _p + 2 + _qk + 1
                a1V[("wrf", _p, _qk)] = sA1_v0 + 4 * _p + 2 + _qk + 1
        for _t in range(NT):
            a1T[("lt", _t)] = sA1_t0 + 32 + _t + 1
            a1V[("lt", _t)] = sA1_v0 + 32 + _t + 1
        with nc.Block() as blk:
            @blk.gpsimd
            def _(g):
                g.load_library(_mlp_lib)
                g.memset(RFBD[:, :, :], 0.0)
                g.drain()
                for p in range(NPAIR):
                    dma(g, qGM, RFBD[0:64, p, 0:64], rfin[2 * p, :, :])
                    dma(g, qGM, RFBD[64:128, p, 64:128], rfin[2 * p + 1, :, :])
            rf_ld = qGM["n"]

            wq_c = {}

            @blk.gpsimd
            def _(g):
                for p in range(NPAIR):
                    if p >= 2:
                        g.wait_ge(mT["h"], a1T[("st", p - 1, 1)])
                    dma(g, qWQ[p % 2], WSLAB[:, p % 2, 0, :], wqkv.ap()[0, 128 * p:128 * (p + 1), :])
                    wq_c[p] = dma(g, qWQ[p % 2], WSLAB[:, p % 2, 1, :], wqkv.ap()[1, 128 * p:128 * (p + 1), :])

            def unit_ps(u):
                return psPK if (u % 2 == 0) else psV

            ukey = {}
            for p in range(NPAIR):
                ukey[2 * p] = ("st", p, 0)
                ukey[2 * p + 1] = ("st", p, 1)
            for t in range(NT):
                ukey[16 + t] = ("lt", t)

            @blk.tensor
            def _(tn):
                tn.wait_ge(qGM["h"], rf_ld)
                tn.wait_ge(qMS["h"], consts_ld)
                for p in range(NPAIR):
                    for qk in range(2):
                        u = 2 * p + qk
                        if qk == 0:
                            tn.wait_ge(qWQ[p % 2]["h"], wq_c[p])
                        if u >= 2:
                            tn.wait_ge(mV["h"], a1V[ukey[u - 2]])
                        wsl = WSLAB[:, p % 2, qk, :].rearrange("q (kc m) -> q kc m", kc=DC, m=128)
                        ps = unit_ps(u)
                        mm = None
                        for kc in range(DC):
                            mm = tn.matmul(ps[:, 128 * kc:128 * (kc + 1)], wsl[:, kc, :], IDB[:, :],
                                           start=True, stop=True)
                        a1T[("st", p, qk)] = bump(mT, mm)
                    for qk in range(2):
                        tn.wait_ge(mV["h"], a1V[("st", p, qk)])
                        if p >= 1:
                            tn.wait_ge(mV["h"], a1V[("wrf", p - 1, qk)])
                        wt = WTSLB[:, p % 2, qk, :].rearrange("dh (kc dd) -> dh kc dd", kc=DC, dd=128)
                        mm = None
                        for dc in range(DC):
                            mm = tn.matmul(psA[:, 1024 * qk + 128 * dc:1024 * qk + 128 * (dc + 1)],
                                           wt[:, dc, :], RFBD[:, p, :], start=True, stop=True)
                        a1T[("wrf", p, qk)] = bump(mT, mm)
                for t in range(NT):
                    u = 16 + t
                    tn.wait_ge(mS["h"], sA0_s0 + t + 1)
                    tn.wait_ge(mV["h"], a1V[ukey[u - 2]])
                    ps = unit_ps(u)
                    mm = None
                    for c in range(DC):
                        mm = tn.matmul(ps[:, 128 * c:128 * (c + 1)], TMA[:, c, t, :], IDB[:, :],
                                       start=True, stop=True)
                    a1T[("lt", t)] = bump(mT, mm)

            @blk.vector
            def _(v):
                w1v = WRF1.rearrange("q dc (a b) -> q dc a b", a=8, b=128)
                w2v = WRF2.rearrange("q dc (a b) -> q dc a b", a=8, b=128)
                for p in range(NPAIR):
                    for qk in range(2):
                        v.wait_ge(mT["h"], a1T[("st", p, qk)])
                        wtd = WTSLB[:, p % 2, qk, :].rearrange("dh (kc dd) -> dh kc dd", kc=DC, dd=128)
                        ps = unit_ps(2 * p + qk)
                        mm = None
                        for kc in range(DC):
                            mm = v.tensor_copy(wtd[:, kc, :], ps[:, 128 * kc:128 * (kc + 1)])
                        a1V[("st", p, qk)] = bump(mV, mm)
                    for qk in range(2):
                        v.wait_ge(mT["h"], a1T[("wrf", p, qk)])
                        wv = w1v if qk == 0 else w2v
                        mm = None
                        for dc in range(DC):
                            mm = v.tensor_copy(wv[:, dc, p, :],
                                               psA[:, 1024 * qk + 128 * dc:1024 * qk + 128 * (dc + 1)])
                        a1V[("wrf", p, qk)] = bump(mV, mm)
                for t in range(NT):
                    v.wait_ge(mT["h"], a1T[("lt", t)])
                    ps = unit_ps(16 + t)
                    fmv = FMA[:, :, :].rearrange("f c (tt p) -> f c tt p", tt=NT, p=128)
                    mm = None
                    for c in range(DC):
                        mm = v.tensor_copy(fmv[:, c, t, :], ps[:, 128 * c:128 * (c + 1)])
                    a1V[("lt", t)] = bump(mV, mm)
        wrf_T = mT["n"]
        wrf_V = mV["n"]
        assert wrf_T == sA1_t0 + 48 and wrf_V == sA1_v0 + 48, (wrf_T, sA1_t0, wrf_V, sA1_v0)

        with nc.Block() as blk:
            @blk.gpsimd
            def _(g):
                for kc in range(DC):
                    dma(g, qGM, WVS[:, kc, :], wqkv.ap()[2, 128 * kc:128 * (kc + 1), :])
        wv_ld = qGM["n"]

        with nc.Block() as blk:
            @blk.gpsimd
            def _(g):
                for j in range(8):
                    dma(g, qPC[j % 2], w2bf[512 * j:512 * (j + 1), :], w2e.ap()[512 * j:512 * (j + 1), :],
                        prewait=False)
                    dma(g, qPC[j % 2], w1bf[512 * j:512 * (j + 1), :], w1e.ap()[512 * j:512 * (j + 1), :],
                        prewait=False)
        wpre = (qPC[0]["n"], qPC[1]["n"])

        sA2_t0 = mT["n"]
        sA2_s0 = mS["n"]
        with nc.Block() as blk:
            @blk.tensor
            def _(tn):
                tn.wait_ge(mV["h"], wrf_V)
                w1v = WRF1.rearrange("q dc (a b) -> q dc a b", a=8, b=128)
                for p in range(NPAIR):
                    if p >= 1:
                        tn.wait_ge(mS["h"], sA2_s0 + p)
                    mm = None
                    for kc in range(DC):
                        for h4 in range(4):
                            mm = tn.matmul(psA[:, 512 * h4:512 * (h4 + 1)], w1v[:, kc, p, :],
                                           FMA[:, kc, 512 * h4:512 * (h4 + 1)],
                                           start=(kc == 0), stop=(kc == DC - 1))
                    bump(mT, mm)

            @blk.scalar
            def _(sc):
                for p in range(NPAIR):
                    sc.wait_ge(mT["h"], sA2_t0 + p + 1)
                    bump(mS, sc.activation(PHQ[:, p, :], psA[:, :], ACT.Relu))
        phiq_T = mT["n"]
        phiq_S = mS["n"]

        sA3_t0 = mT["n"]
        sA3_s0 = mS["n"]
        sA3_v0 = mV["n"]
        sA3_g0 = mG["n"]
        with nc.Block() as blk:
            @blk.gpsimd
            def _(g):
                g.wait_ge(mV["h"], wrf_V)
                for b in range(2):
                    vv = VAS[:, b, :].rearrange("q (a c) -> q a c", a=8, c=130)
                    g.memset(vv[:, :, 128:129], 1.0)
                    g.memset(vv[:, :, 129:130], 0.0)
                g.drain()
                bump(mG, g.drain())

            @blk.tensor
            def _(tn):
                tn.wait_ge(mS["h"], phiq_S)
                tn.wait_ge(qGM["h"], wv_ld)
                tn.wait_ge(mG["h"], sA3_g0 + 1)
                for t in range(NT):
                    b = t % 2
                    if t >= 2:
                        tn.wait_ge(mS["h"], sA3_s0 + (t - 1))
                        tn.wait_ge(mV["h"], sA3_v0 + 2 * (t - 2) + 1)
                    mm = None
                    for kc in range(DC):
                        lhsT = FMA[:, kc, :].rearrange("f (t p) -> f t p", t=NT, p=128)[:, t, :]
                        for h2 in range(2):
                            tn.matmul(psPK[:, 512 * h2:512 * (h2 + 1)], lhsT,
                                      WRF2[:, kc, 512 * h2:512 * (h2 + 1)],
                                      start=(kc == 0), stop=(kc == DC - 1))
                            mm = tn.matmul(psV[:, 512 * h2:512 * (h2 + 1)], lhsT,
                                           WVS[:, kc, 512 * h2:512 * (h2 + 1)],
                                           start=(kc == 0), stop=(kc == DC - 1))
                    bump(mT, mm)
                    tn.wait_ge(mS["h"], sA3_s0 + t + 1)
                    tn.wait_ge(mV["h"], sA3_v0 + 2 * t + 1)
                    if t >= 1:
                        tn.wait_ge(mV["h"], sA3_v0 + 2 * (t - 1) + 2)
                    mm = None
                    for p in range(NPAIR):
                        mm = tn.matmul(psA[:, 256 * p:256 * p + 130],
                                       PKS[:, b, :].rearrange("q (a c) -> q a c", a=8, c=128)[:, p, :],
                                       VAS[:, b, :].rearrange("q (a c) -> q a c", a=8, c=130)[:, p, :],
                                       start=True, stop=True)
                    bump(mT, mm)

            @blk.scalar
            def _(sc):
                for t in range(NT):
                    sc.wait_ge(mT["h"], sA3_t0 + 2 * t + 1)
                    if t >= 2:
                        sc.wait_ge(mT["h"], sA3_t0 + 2 * (t - 1) + 2)
                    bump(mS, sc.activation(PKS[:, t % 2, :], psPK[:, :], ACT.Relu))

            @blk.vector
            def _(v):
                kvsrc = psA.ap().rearrange("q (a c) -> q a c", a=8, c=256)[:, :, 0:130]
                for t in range(NT):
                    v.wait_ge(mT["h"], sA3_t0 + 2 * t + 1)
                    if t >= 2:
                        v.wait_ge(mT["h"], sA3_t0 + 2 * (t - 1) + 2)
                    vv = VAS[:, t % 2, :].rearrange("q (a c) -> q a c", a=8, c=130)
                    v.tensor_copy(vv[:, :, 0:128], psV.ap().rearrange("q (a c) -> q a c", a=8, c=128))
                    v.drain()
                    bump(mV, v.drain())
                    v.wait_ge(mT["h"], sA3_t0 + 2 * t + 2)
                    if t == 0:
                        v.tensor_copy(KVACC[:, :, :], kvsrc)
                    else:
                        v.tensor_tensor(KVACC[:, :, :], KVACC[:, :, :], kvsrc, AL.add)
                    v.drain()
                    bump(mV, v.drain())
        tile_T = mT["n"]

        with nc.Block() as blk:
            @blk.vector
            def _(v):
                v.wait_ge(qMS["h"], consts_ld)
                v.tensor_copy(RWHI[:, :, :], RWF[:, :, :])
                v.drain()
                mm = None
                for kc in range(DC):
                    mm = v.tensor_tensor(RWLO[:, kc, :], RWF[:, kc, :], RWHI[:, kc, :], AL.subtract)
                bump(mV, mm)
        rwc_V = mV["n"]

        with nc.Block() as blk:
            @blk.vector
            def _(v):
                kvv = KVACC
                kcat = KVCAT[:, 0, :].rearrange("q (a c) -> q a c", a=8, c=65)
                for p in range(NPAIR):
                    v.tensor_copy(kcat[0:64, p, 0:64], kvv[0:64, p, 0:64])
                    v.tensor_copy(kcat[0:64, p, 64:65], kvv[0:64, p, 128:129])
                    v.tensor_copy(kcat[64:128, p, 0:64], kvv[64:128, p, 64:128])
                    v.tensor_copy(kcat[64:128, p, 64:65], kvv[64:128, p, 128:129])
                v.drain()
                bump(mV, v.drain())
            kvpack_V = mV["n"]

            @blk.sync
            def _(sy):
                sy.wait_ge(mV["h"], kvpack_V)
                dma(sy, qMS, kvb_i[:, :], KVCAT[:, 0, :])
            kv_st = qMS["n"]

            @blk.gpsimd
            def _(g):
                g.wait_ge(qMS["h"], kv_st)
                g.collective_compute(
                    "AllReduce", AL.add,
                    replica_groups=[[0, 1, 2, 3], [4, 5, 6, 7]],
                    ins=[kvb_i.ap().opt()], outs=[kvb_o.ap().opt()],
                ).then_inc(sCC["h"], 1)
                sCC["n"] += 1
            kv_cc = sCC["n"]

            @blk.sync
            def _(sy):
                sy.wait_ge(sCC["h"], kv_cc)
                dma(sy, qMS, KVAR[:, 0, :], kvb_o[:, :])
            kv_ld = qMS["n"]

            @blk.vector
            def _(v):
                v.wait_ge(qMS["h"], kv_ld)
                v.memset(KVAUG[:, :, :], 0.0)
                v.drain()
                kar = KVAR[:, 0, :].rearrange("q (a c) -> q a c", a=8, c=65)
                for p in range(NPAIR):
                    v.tensor_copy(KVAUG[0:64, p, 0:65], kar[0:64, p, :])
                    v.tensor_copy(KVAUG[64:128, p, 65:130], kar[64:128, p, :])
                v.drain()
                bump(mV, v.drain())
        kvaug_V = mV["n"]

        sC1_t0 = mT["n"]
        sC1_v0 = mV["n"]

        def outps(t):
            if t % 2 == 0:
                return [psA[:, 256 * p:256 * p + 130] for p in range(8)]
            return [psPK[:, 256 * p:256 * p + 130] for p in range(4)] + \
                   [psV[:, 256 * (p - 4):256 * (p - 4) + 130] for p in range(4, 8)]

        with nc.Block() as blk:
            @blk.tensor
            def _(tn):
                tn.wait_ge(mV["h"], kvaug_V)
                for t in range(NT):
                    if t >= 2:
                        tn.wait_ge(mV["h"], sC1_v0 + (t - 1))
                    pss = outps(t)
                    mm = None
                    for p in range(NPAIR):
                        mm = tn.matmul(pss[p],
                                       PHQ[:, p, :].rearrange("q (tt c) -> q tt c", tt=NT, c=128)[:, t, :],
                                       KVAUG[:, p, :], start=True, stop=True)
                    bump(mT, mm)

            @blk.vector
            def _(v):
                for t in range(NT):
                    v.wait_ge(mT["h"], sC1_t0 + t + 1)
                    pss = outps(t)
                    if t % 2 == 0:
                        nv1 = psA.ap().rearrange("q (a c) -> q a c", a=8, c=256)[:, :, 64]
                        nv2 = psA.ap().rearrange("q (a c) -> q a c", a=8, c=256)[:, :, 129]
                        v.tensor_scalar(SCC[:, 0:8], nv1, STAB, None, AL.add)
                        v.tensor_scalar(SCC[:, 8:16], nv2, STAB, None, AL.add)
                        v.drain()
                    else:
                        nva = psPK.ap().rearrange("q (a c) -> q a c", a=4, c=256)
                        nvb = psV.ap().rearrange("q (a c) -> q a c", a=4, c=256)
                        v.tensor_scalar(SCC[:, 0:4], nva[:, :, 64], STAB, None, AL.add)
                        v.tensor_scalar(SCC[:, 4:8], nvb[:, :, 64], STAB, None, AL.add)
                        v.tensor_scalar(SCC[:, 8:12], nva[:, :, 129], STAB, None, AL.add)
                        v.tensor_scalar(SCC[:, 12:16], nvb[:, :, 129], STAB, None, AL.add)
                        v.drain()
                    v.reciprocal(SCC[:, 0:16], SCC[:, 0:16])
                    v.drain()
                    for p in range(NPAIR):
                        if t % 2 == 0:
                            sa, sb = p, 8 + p
                        else:
                            sa = p if p < 4 else 4 + (p - 4)
                            sb = (8 + p) if p < 4 else 12 + (p - 4)
                        v.tensor_scalar(TMA[:, p, t, 0:64], pss[p][:, 0:64], SCC[:, sa:sa + 1], None, AL.mult)
                        v.tensor_scalar(TMA[:, p, t, 64:128], pss[p][:, 65:129], SCC[:, sb:sb + 1], None, AL.mult)
                    v.drain()
                    bump(mV, v.drain())
        attnpre_V = mV["n"]

        with nc.Block() as blk:
            @blk.gpsimd
            def _(g):
                g.wait_ge(mS["h"], phiq_S)
                for kc in range(DC):
                    dma(g, qGM, WOS[:, kc, :], wout.ap()[128 * kc:128 * (kc + 1), :])
                g.wait_ge(mV["h"], kvaug_V)
                dma(g, qGM, BROW, bout.ap().rearrange("(x d) -> x d", x=1))
            wo_ld = qGM["n"]

            @blk.scalar
            def _(sc):
                sc.wait_ge(mV["h"], attnpre_V)
                for c in range(DC):
                    o = FMA[:, c, :].rearrange("f (t p) -> f t p", t=NT, p=128)
                    i = TMA[:, c, :, :]
                    dma(sc, qTR[c % 2], o, i, transpose=True)
            tra = (qTR[0]["n"], qTR[1]["n"])

        sC2_t0 = mT["n"]
        sC2_v0 = mV["n"]
        sC2_s0 = mS["n"]

        def c2v(t, j):
            return sC2_v0 + 1 + 3 * t + j

        with nc.Block() as blk:
            @blk.tensor
            def _(tn):
                tn.wait_ge(qGM["h"], wo_ld)
                tn.wait_ge(mV["h"], attnpre_V)
                tn.matmul(psPK[:, 0:512], ONER[:, :], BROW[:, 0:512], start=True, stop=True)
                mm = tn.matmul(psPK[:, 512:1024], ONER[:, :], BROW[:, 512:1024], start=True, stop=True)
                bump(mT, mm)
                tn.wait_ge(qTR[0]["h"], tra[0])
                tn.wait_ge(qTR[1]["h"], tra[1])
                for t in range(NT):
                    ps = psV if (t % 2 == 0) else psPK
                    if t == 1:
                        tn.wait_ge(mV["h"], sC2_v0 + 1)
                    if t >= 2:
                        tn.wait_ge(mV["h"], c2v(t - 2, 1))
                    mm = None
                    for kc in range(DC):
                        lhsT = FMA[:, kc, :].rearrange("f (t p) -> f t p", t=NT, p=128)[:, t, :]
                        for h2 in range(2):
                            mm = tn.matmul(ps[:, 512 * h2:512 * (h2 + 1)], lhsT,
                                           WOS[:, kc, 512 * h2:512 * (h2 + 1)],
                                           start=(kc == 0), stop=(kc == DC - 1))
                    bump(mT, mm)

            xld_cnt = {}
            spill_cnt = {}

            @blk.sync
            def _(sy):
                def spill(j):
                    dma(sy, qSP[j % 2], x2d[128 * j:128 * (j + 1), :], XNEW[:, j % 2, :])
                    dma(sy, qSP[j % 2], hb[128 * j:128 * (j + 1), :].rearrange("p (c f) -> p c f", c=DC, f=128), TMA[:, :, j, :])
                    spill_cnt[j] = qSP[j % 2]["n"]
                for t in range(NT):
                    if t >= 2:
                        sy.wait_ge(mV["h"], c2v(t - 2, 3))
                    xld_cnt[t] = dma(sy, qXL[t % 2], XSTR[:, t % 2, :], xin[128 * t:128 * (t + 1), :])
                    if t >= 2:
                        spill(t - 2)
                for j in (NT - 2, NT - 1):
                    sy.wait_ge(mV["h"], c2v(j, 3))
                    spill(j)

            @blk.vector
            def _(v):
                v.wait_ge(mT["h"], sC2_t0 + 1)
                v.tensor_copy(BCB[:, 0, :], psPK[:, :])
                v.drain()
                bump(mV, v.drain())
                for t in range(NT):
                    ps = psV if (t % 2 == 0) else psPK
                    v.wait_ge(mT["h"], sC2_t0 + 1 + t + 1)
                    v.wait_ge(qXL[t % 2]["h"], xld_cnt[t])
                    if t >= 2:
                        v.wait_ge(mS["h"], sC2_s0 + (t - 2) + 1)
                        v.wait_ge(qSP[t % 2]["h"], spill_cnt[t - 2])
                    v.tensor_add(XNEW[:, t % 2, :], ps[:, :], BCB[:, 0, :])
                    v.drain()
                    v.tensor_add(XNEW[:, t % 2, :], XNEW[:, t % 2, :], XSTR[:, t % 2, :])
                    v.drain()
                    v.bn_stats(STATS[:, 0:6], XNEW[:, t % 2, 0:512])
                    v.bn_stats(STATS[:, 6:12], XNEW[:, t % 2, 512:1024])
                    v.drain()
                    v.bn_aggr(MRS[:, t, 0:2], STATS[:, :])
                    v.drain()
                    v.tensor_scalar(MRS[:, t, 2:3], MRS[:, t, 1:2], 1.0, EPS_LN, AL.mult, AL.add)
                    v.drain()
                    v.reciprocal(MRS[:, t, 3:4], MRS[:, t, 2:3])
                    v.drain()
                    bump(mV, v.drain())
                    v.wait_ge(mS["h"], sC2_s0 + t + 1)
                    v.tensor_scalar(XSTR[:, t % 2, :], XNEW[:, t % 2, :], MRS[:, t, 2:3], MRS[:, t, 3:4], AL.mult, AL.add)
                    v.drain()
                    bump(mV, v.drain())
                    v.tensor_tensor(HLOT[:, :, t, :], XSTR[:, t % 2, :], TMA[:, :, t, :], AL.subtract)
                    v.drain()
                    bump(mV, v.drain())

            @blk.scalar
            def _(sc):
                sc.wait_ge(qTR[0]["h"], tra[0])
                sc.wait_ge(qTR[1]["h"], tra[1])
                for t in range(NT):
                    sc.wait_ge(mV["h"], c2v(t, 1))
                    sc.activation(MRS[:, t, 2:3], MRS[:, t, 3:4], ACT.Sqrt)
                    sc.drain()
                    sc.mul(MRS[:, t, 3:4], MRS[:, t, 0:1], MRS[:, t, 2:3])
                    sc.drain()
                    sc.mul(MRS[:, t, 3:4], MRS[:, t, 3:4], -1.0)
                    sc.drain()
                    sc.activation(TMA[:, :, t, :], XNEW[:, t % 2, :], ACT.Identity,
                                  bias=MRS[:, t, 3:4], scale=MRS[:, t, 2:3])
                    sc.drain()
                    bump(mS, sc.drain())
        hspill = (qSP[0]["n"], qSP[1]["n"])
        hln_S = mS["n"]
        hlo_V = mV["n"]

        if _STAGE == "attn":
            with nc.Block() as blk:
                @blk.sync
                def _(sy):
                    sy.wait_ge(qSP[0]["h"], hspill[0])
                    sy.wait_ge(qSP[1]["h"], hspill[1])
                    for t in range(NT):
                        dma(sy, qST[t % 2], out[128 * t:128 * (t + 1), :], x2d[128 * t:128 * (t + 1), :])
                    sy.wait_ge(qST[0]["h"], qST[0]["n"])
                    sy.wait_ge(qST[1]["h"], qST[1]["n"])
            return nc

        with nc.Block() as blk:
            @blk.gpsimd
            def _(g):
                g.wait_ge(qSP[0]["h"], hspill[0])
                g.wait_ge(qSP[1]["h"], hspill[1])
                g.collective_compute(
                    "AllGather", AL.bypass,
                    replica_groups=[list(range(NCORES))],
                    ins=[hb.ap().opt()], outs=[hall.ap().opt()],
                ).then_inc(sCC["h"], 1)
                sCC["n"] += 1
            hall_cc = sCC["n"]

            @blk.scalar
            def _(sc):
                sc.wait_ge(mS["h"], hln_S)
                sc.wait_ge(qSP[0]["h"], hspill[0])
                sc.wait_ge(qSP[1]["h"], hspill[1])
                for c in range(DC):
                    o = FMA[:, c, :].rearrange("f (t p) -> f t p", t=NT, p=128)
                    i = TMA[:, c, :, :]
                    dma(sc, qTR[c % 2], o, i, transpose=True)
            trh = (qTR[0]["n"], qTR[1]["n"])

            @blk.scalar
            def _(sc):
                sc.wait_ge(mV["h"], hlo_V)
                for c in range(DC):
                    o = HLOF[:, c, :].rearrange("f (t p) -> f t p", t=NT, p=128)
                    i = HLOT[:, c, :, :]
                    dma(sc, qTR[c % 2], o, i, transpose=True)
            trh2 = (qTR[0]["n"], qTR[1]["n"])

        sE_t0 = mT["n"]
        sE_v0 = mV["n"]
        sE_s0 = mS["n"]
        with nc.Block() as blk:
            @blk.tensor
            def _(tn):
                tn.wait_ge(qTR[0]["h"], trh2[0])
                tn.wait_ge(qTR[1]["h"], trh2[1])
                tn.wait_ge(mV["h"], rwc_V)
                first = True
                for ph in range(3):
                    for kc in range(DC):
                        lhsT = [RWHI[:, kc, :], RWLO[:, kc, :], RWHI[:, kc, :]][ph]
                        rhs = [FMA, FMA, HLOF][ph][:, kc, :]
                        last = (ph == 2 and kc == DC - 1)
                        for h4 in range(4):
                            mm = tn.matmul(psA[0:8, 512 * h4:512 * (h4 + 1)], lhsT,
                                           rhs[:, 512 * h4:512 * (h4 + 1)],
                                           start=first, stop=last)
                        first = False
                bump(mT, mm)
            lg_T = mT["n"]

            @blk.scalar
            def _(sc):
                sc.wait_ge(mT["h"], lg_T)
                sc.activation(LGT[0:8, 0, :], psA[0:8, :], ACT.Identity, bias=RWBS[:, 0:1])
                sc.drain()
                bump(mS, sc.drain())
            lg_S = mS["n"]

            @blk.tensor
            def _(tn):
                tn.wait_ge(mS["h"], lg_S)
                for t in range(NT):
                    ps = psPK if (t % 2 == 0) else psV
                    if t >= 2:
                        tn.wait_ge(mV["h"], sE_v0 + (t - 1))
                    bump(mT, tn.transpose(ps[0:128, 0:8], LGT[0:8, 0, 128 * t:128 * (t + 1)], ID8S[:, :]))

            @blk.vector
            def _(v):
                for t in range(NT):
                    ps = psPK if (t % 2 == 0) else psV
                    v.wait_ge(mT["h"], lg_T + t + 1)
                    v.tensor_copy(LTM[:, t, :], ps[0:128, 0:8])
                    v.drain()
                    bump(mV, v.drain())
            ltm_V = mV["n"]

            @blk.vector
            def _(v):
                for t in range(NT):
                    v.max(out=SC8[:, :], in_=LTM[:, t, :])
                    v.drain()
                    v.tensor_copy(M12[:, t, 0:2], SC8[:, 0:2])
                    v.drain()
                    v.tensor_scalar(RTS[:, t, 0:1], SC8[:, 0:1], -1.0, None, AL.mult)
                    v.drain()
                    v.tensor_scalar(MSK[:, t, :], LTM[:, t, :], SC8[:, 1:2], None, AL.is_ge)
                    v.drain()
                    bump(mV, v.drain())
            msk_V = mV["n"]

            @blk.scalar
            def _(sc):
                for t in range(NT):
                    sc.wait_ge(mV["h"], ltm_V + t + 1)
                    sc.activation(E8T[:, t, :], LTM[:, t, :], ACT.Exp, bias=RTS[:, t, 0:1])
                    sc.drain()
                    bump(mS, sc.drain())

            @blk.vector
            def _(v):
                for t in range(NT):
                    v.wait_ge(mS["h"], lg_S + t + 1)
                    v.reduce_sum(RTS[:, t, 1:2], E8T[:, t, :], axis=AXX)
                    v.drain()
                    v.reciprocal(RTS[:, t, 2:3], RTS[:, t, 1:2])
                    v.drain()
                    v.tensor_scalar(GTS[:, t, :], E8T[:, t, :], RTS[:, t, 2:3], None, AL.mult)
                    v.drain()
                    bump(mV, v.drain())
            gts_V = mV["n"]

            @blk.tensor
            def _(tn):
                tn.wait_ge(mV["h"], msk_V)
                mm = None
                for t in range(NT):
                    mm = tn.matmul(psA[0:8, 2000 + t:2001 + t], MSK[:, t, :], ONEC[:, :], start=True, stop=True)
                bump(mT, mm)
            cntT_T = mT["n"]

            @blk.vector
            def _(v):
                v.wait_ge(mT["h"], cntT_T)
                v.tensor_copy(CNTT[:, :], psA[0:8, 2000:2016])
                v.drain()
                bump(mV, v.drain())
            cntT_V = mV["n"]

            @blk.tensor
            def _(tn):
                tn.wait_ge(mV["h"], cntT_V)
                tn.wait_ge(mV["h"], msk_V)
                bump(mT, tn.transpose(psV[0:16, 0:8], CNTT[:, :], ID8S[:, :]))
            cnts_T = mT["n"]

            @blk.vector
            def _(v):
                v.wait_ge(mT["h"], cnts_T)
                v.tensor_copy(CNTS[:, :], psV[0:16, 0:8])
                v.drain()
                bump(mV, v.drain())
            cnts_V = mV["n"]

            ecnt = {}

            @blk.sync
            def _(sy):
                sy.wait_ge(mV["h"], cnts_V)
                ecnt["cnt_st"] = dma(sy, qMS, cnt_i[:, :], CNTS[:, :])
                dma(sy, qMS, TSEL, tsel[:, :])
                dma(sy, qMS, TRIS, tri[:, :])
            tsel_ld = qMS["n"]
            cnt_st = ecnt["cnt_st"]

            @blk.gpsimd
            def _(g):
                g.wait_ge(qMS["h"], cnt_st)
                g.collective_compute(
                    "AllGather", AL.bypass,
                    replica_groups=[list(range(NCORES))],
                    ins=[cnt_i.ap().opt()], outs=[cnt_o.ap().opt()],
                ).then_inc(sCC["h"], 1)
                sCC["n"] += 1
            cnt_cc = sCC["n"]

            @blk.sync
            def _(sy):
                sy.wait_ge(sCC["h"], cnt_cc)
                dma(sy, qMS, CAG[:, :], cnt_o[:, :])
            cag_ld = qMS["n"]

            @blk.tensor
            def _(tn):
                tn.wait_ge(qMS["h"], cag_ld)
                tn.wait_ge(qMS["h"], tsel_ld)
                bump(mT, tn.matmul(psPK[:, 0:8], TRIS, CAG[:, :], start=True, stop=True))
            gb_T = mT["n"]

            @blk.vector
            def _(v):
                v.wait_ge(mT["h"], gb_T)
                v.tensor_copy(GBS[:, :], psPK[:, 0:8])
                v.drain()
                bump(mV, v.drain())
            gb_V = mV["n"]

            @blk.tensor
            def _(tn):
                tn.wait_ge(mV["h"], gb_V)
                bump(mT, tn.matmul(psV[0:16, 0:8], SELS[:, :], GBS[:, :], start=True, stop=True))
            mb_T = mT["n"]

            @blk.vector
            def _(v):
                v.wait_ge(mT["h"], mb_T)
                v.tensor_copy(MBS[:, :], psV[0:16, 0:8])
                v.drain()
                bump(mV, v.drain())
            mb_V = mV["n"]

            @blk.tensor
            def _(tn):
                tn.wait_ge(mV["h"], mb_V)
                tn.wait_ge(qMS["h"], tsel_ld)
                for t in range(NT):
                    ps = psPK if (t % 2 == 0) else psV
                    if t >= 2:
                        tn.wait_ge(mV["h"], mb_V + (t - 1))
                    tn.matmul(ps[:, 0:8], TRIS, MSK[:, t, :], start=True, stop=False)
                    bump(mT, tn.matmul(ps[:, 0:8], TSEL[:, 128 * t:128 * (t + 1)], MBS[:, :], start=False, stop=True))

            @blk.vector
            def _(v):
                for t in range(NT):
                    ps = psPK if (t % 2 == 0) else psV
                    v.wait_ge(mT["h"], mb_T + t + 1)
                    v.tensor_copy(POS[:, t, :], ps[:, 0:8])
                    v.drain()
                    bump(mV, v.drain())
            pos_V = mV["n"]

            @blk.vector
            def _(v):
                for t in range(NT):
                    v.tensor_scalar(SC8[:, :], POS[:, t, :], float(CAP), None, AL.is_lt)
                    v.drain()
                    v.tensor_tensor(MSK[:, t, :], MSK[:, t, :], SC8[:, :], AL.mult)
                    v.drain()
                    v.tensor_tensor(WGT[:, t, :], GTS[:, t, :], MSK[:, t, :], AL.mult)
                    v.drain()
                    for k in range(2):
                        oi = OI1 if k == 0 else OI2
                        v.tensor_scalar(SC8[:, :], LTM[:, t, :], M12[:, t, k:k + 1], None, AL.is_equal)
                        v.drain()
                        v.tensor_tensor(SC8B[:, :], SC8[:, :], IOTS[:, :], AL.mult)
                        v.drain()
                        v.reduce_sum(EPW[:, 2 * k, t:t + 1], SC8B[:, :], axis=AXX)
                        v.drain()
                        v.tensor_tensor(SC8B[:, :], SC8[:, :], POS[:, t, :], AL.mult)
                        v.drain()
                        v.reduce_sum(EPW[:, 2 * k + 1, t:t + 1], SC8B[:, :], axis=AXX)
                        v.drain()
                        v.tensor_tensor(SC8B[:, :], SC8[:, :], WGT[:, t, :], AL.mult)
                        v.drain()
                        v.reduce_sum(WCOL[:, k, t:t + 1], SC8B[:, :], axis=AXX)
                        v.drain()
                        v.tensor_scalar(SCC[:, 0:1], EPW[:, 2 * k + 1, t:t + 1], float(CAP + 2 * t + k), None, AL.min)
                        v.drain()
                        v.tensor_copy(OIS[:, 0:1], SCC[:, 0:1])
                        v.drain()
                        v.tensor_scalar(OIS[:, 1:2], OIS[:, 0:1], 4, None, AL.arith_shift_right)
                        v.tensor_scalar(OIS[:, 2:3], OIS[:, 0:1], 15, None, AL.bitwise_and)
                        v.drain()
                        v.tensor_scalar(OIS[:, 2:3], OIS[:, 2:3], ILC, None, AL.mult)
                        v.tensor_copy(OIS[:, 3:4], EPW[:, 2 * k, t:t + 1])
                        v.drain()
                        v.tensor_scalar(OIS[:, 3:4], OIS[:, 3:4], ESTR, None, AL.mult)
                        v.drain()
                        v.tensor_tensor(OIS[:, 3:4], OIS[:, 3:4], OIS[:, 2:3], AL.add)
                        v.drain()
                        v.tensor_tensor(oi[:, t:t + 1], OIS[:, 3:4], OIS[:, 1:2], AL.add)
                        v.drain()
                        v.tensor_scalar(SCC[:, 4:5], EPW[:, 2 * k + 1, t:t + 1], float(CAP - 1), None, AL.min)
                        v.drain()
                        v.tensor_copy(OIS[:, 0:1], SCC[:, 4:5])
                        v.drain()
                        v.tensor_scalar(OIS[:, 1:2], OIS[:, 0:1], 9, None, AL.arith_shift_right)
                        v.tensor_copy(OIS[:, 2:3], EPW[:, 2 * k, t:t + 1])
                        v.drain()
                        v.tensor_scalar(OIS[:, 1:2], OIS[:, 1:2], 3584, None, AL.mult)
                        v.tensor_scalar(OIS[:, 2:3], OIS[:, 2:3], 512, None, AL.mult)
                        v.drain()
                        v.tensor_tensor(OIS[:, 0:1], OIS[:, 0:1], OIS[:, 1:2], AL.add)
                        v.drain()
                        v.tensor_tensor(OIS[:, 0:1], OIS[:, 0:1], OIS[:, 2:3], AL.add)
                        v.drain()
                        v.tensor_copy(YOF[:, k, t:t + 1], OIS[:, 0:1])
                        v.drain()
                    bump(mV, v.drain())
            offs_V = mV["n"]

            @blk.gpsimd
            def _(g):
                g.memset(ZTAB, 0.0)
                g.drain()
                dma(g, qSC, tab_i.ap().rearrange("(p c) x -> p (c x)", p=128), ZTAB)
                g.wait_ge(qSC["h"], qSC["n"])
                for t in range(NT):
                    g.wait_ge(mV["h"], pos_V + t + 1)
                    for oi in (OI1, OI2):
                        g.wait_ge(qSC["h"], qSC["n"])
                        g.indirect_dma_start(
                            out=tab_i.ap(), out_offset=IndirectOffsetOnAxis(ap=oi[:, t:t + 1], axis=0),
                            in_=TOKF[:, t:t + 1], in_offset=None,
                        ).then_inc(qSC["h"], 16)
                        qSC["n"] += 16
                g.wait_ge(qSC["h"], qSC["n"])
                g.collective_compute(
                    "ReduceScatter", AL.add,
                    replica_groups=[list(range(NCORES))],
                    ins=[tab_i.ap().opt()], outs=[tab_o.ap().opt()],
                ).then_inc(sCC["h"], 1)
                sCC["n"] += 1
            tab_cc = sCC["n"]

            @blk.sync
            def _(sy):
                sy.wait_ge(mV["h"], offs_V)
                dma(sy, qMS, yd1.ap().rearrange("(p t) -> p t", p=128), YOF[:, 0, :])
                dma(sy, qMS, yd2.ap().rearrange("(p t) -> p t", p=128), YOF[:, 1, :])

            @blk.sync
            def _(sy):
                src1 = bass.AP(yd1, 0, [[16, 16], [1, 16], [256, 8]])
                src2 = bass.AP(yd2, 0, [[16, 16], [1, 16], [256, 8]])
                dma(sy, qMS, YIFA.rearrange("q (t ph) -> q t ph", t=16, ph=8), src1, allow_slow_non_contiguous=True)
                dma(sy, qMS, YIFB.rearrange("q (t ph) -> q t ph", t=16, ph=8), src2, allow_slow_non_contiguous=True)
            yif_ld = qMS["n"]

            @blk.vector
            def _(v):
                v.wait_ge(qMS["h"], yif_ld)
                v.memset(YI16A[:, :], 0)
                v.memset(YI16B[:, :], 0)
                v.drain()
                v.tensor_copy(YI16A[0:16, :], YIFA)
                v.tensor_copy(YI16B[0:16, :], YIFB)
                v.drain()
                bump(mV, v.drain())
            yi16_V = mV["n"]

            @blk.sync
            def _(sy):
                sy.wait_ge(mV["h"], yi16_V)
                dma(sy, qMS, YI16A[16:32, :], YI16A[0:16, :])
                dma(sy, qMS, YI16B[16:32, :], YI16B[0:16, :])
            yi16_dup = qMS["n"]

            @blk.sync
            def _(sy):
                sy.wait_ge(sCC["h"], tab_cc)
                dma(sy, qMS, XIF, tab_o.ap().rearrange("(q c) x -> q (c x)", q=16))
            xif_ld = qMS["n"]

            @blk.vector
            def _(v):
                v.wait_ge(qMS["h"], xif_ld)
                v.memset(XI16[:, :], 0)
                v.drain()
                v.tensor_copy(XI16[0:16, :], XIF)
                v.drain()
                bump(mV, v.drain())
            xi16_Vl = mV["n"]

            @blk.sync
            def _(sy):
                sy.wait_ge(mV["h"], xi16_Vl)
                dma(sy, qMS, XI16[16:32, :], XI16[0:16, :])
            xi16_dup = qMS["n"]
        xi16_V = mV["n"]

        if _STAGE == "router":
            fmadbg = nc.dram_tensor("fmadbg", [128, DC, 2048], BF16, kind="ExternalOutput")
            hlodbg = nc.dram_tensor("hlodbg", [128, DC, 2048], BF16, kind="ExternalOutput")
            lgtdbg = nc.dram_tensor("lgtdbg", [8, 2048], F32, kind="ExternalOutput")
            ltmdbg2 = nc.dram_tensor("ltmdbg2", [128, NT, NEXP], F32, kind="ExternalOutput")
            with nc.Block() as blk:
                @blk.sync
                def _(sy):
                    sy.wait_ge(mV["h"], offs_V)
                    dma(sy, qMS, fmadbg[:, :, :], FMA[:, :, :])
                    dma(sy, qMS, hlodbg[:, :, :], HLOF[:, :, :])
                    dma(sy, qMS, lgtdbg[:, :], LGT[0:8, 0, :])
                    dma(sy, qMS, ltmdbg2[:, :, :], LTM[:, :, :])
                    sy.wait_ge(qMS["h"], qMS["n"])
            return nc

        sF_v0 = mV["n"]
        sF_s0 = mS["n"]
        with nc.Block() as blk:
            @blk.gpsimd
            def _(g):
                dma(g, qGM, BROW, b2e.ap().rearrange("(x d) -> x d", x=1))
            b2_ld = qGM["n"]

            @blk.tensor
            def _(tn):
                tn.wait_ge(qGM["h"], b2_ld)
                tn.wait_ge(mV["h"], pos_V)
                tn.matmul(psPK[:, 0:512], ONER[:, :], BROW[:, 0:512], start=True, stop=True)
                bump(mT, tn.matmul(psPK[:, 512:1024], ONER[:, :], BROW[:, 512:1024], start=True, stop=True))
            b2b_T = mT["n"]

            @blk.vector
            def _(v):
                v.wait_ge(mT["h"], b2b_T)
                v.tensor_copy(BCB[:, 0, :], psPK[:, :])
                v.drain()
                bump(mV, v.drain())
            b2b_V = mV["n"]

        F_t0 = mT["n"]

        def f_mm1(c, m):
            return F_t0 + 2 + 66 * c + m + 1

        def f_xtrA(c):
            return F_t0 + 1 if c == 0 else F_t0 + 2 + 66 * (c - 1) + 33

        def f_xtrB(c):
            return F_t0 + 2 if c == 0 else F_t0 + 2 + 66 * (c - 1) + 34

        def f_mm2(c, m):
            return F_t0 + 2 + 66 * c + 34 + m + 1

        F_s0 = mS["n"]

        def f_gelu(c, m):
            return F_s0 + 32 * c + m + 1

        F_v0 = mV["n"]

        def f_xtrAv(c):
            return F_v0 + 1 if c == 0 else F_v0 + 2 + 6 * (c - 1) + 1

        def f_xtrBv(c):
            return F_v0 + 2 if c == 0 else F_v0 + 2 + 6 * (c - 1) + 2

        def f_eout(c, rt):
            return F_v0 + 2 + 6 * c + 2 + rt + 1

        F_mg0 = mG["n"]
        xg_c = {}
        w1_c = {}

        def emit_xtr_mms(tn, c, batch):
            mm = None
            for j in range(16):
                rb, fq = j // 4, j % 4
                fc = 4 * batch + fq
                mm = tn.matmul(psA[:, 128 * j:128 * (j + 1)],
                               XTM[:, rb, 128 * fc:128 * (fc + 1)], IDB[:, :],
                               start=True, stop=True)
            bump(mT, mm)

        def emit_xtr_evac(v, c, batch):
            dstv = XG[:, 4 * batch:4 * batch + 4, :].rearrange("p fq (rb q) -> p fq rb q", rb=4, q=128)
            srcv = bass.AP(psA, 0, [[2048, 128], [128, 4], [512, 4], [1, 128]])
            v.tensor_copy(dstv, srcv)
            v.drain()
            bump(mV, v.drain())

        with nc.Block() as blk:
            @blk.gpsimd
            def _(g):
                g.wait_ge(sCC["h"], hall_cc)
                g.wait_ge(mV["h"], xi16_V)
                g.wait_ge(qMS["h"], xi16_dup)
                for c in range(NCH):
                    if c >= 1:
                        g.wait_ge(mT["h"], f_xtrB(c - 1))
                    qq = qGA[c % 2]
                    if qq["n"] > 0:
                        g.wait_ge(qq["h"], qq["n"])
                    g.dma_gather(XTM[:, :, :], hall.ap(), XI16[:, 32 * c:32 * (c + 1)],
                                 num_idxs=RCH, num_idxs_reg=RCH, elem_size=D).then_inc(qq["h"], 16)
                    qq["n"] += 16
                    xg_c[c] = qq["n"]
                    bump(mG, g.drain())
                    if c == 0:
                        g.wait_ge(qPC[0]["h"], wpre[0])
                        g.wait_ge(qPC[1]["h"], wpre[1])
                    for m in range(MHID):
                        if 32 * c + m >= 4:
                            pc, pm = (c, m - 4) if m >= 4 else (c - 1, m + 28)
                            g.wait_ge(mT["h"], f_mm1(pc, pm))
                        w1_c[(c, m)] = dma(g, qWQ[m % 2], W1S[:, m % 4, :, :],
                                           w1bf[128 * m:128 * (m + 1), :].rearrange("q (kc mm) -> q kc mm", kc=DC, mm=128))
            w2cnt = {}
            estcnt = {}

            @blk.sync
            def _(sy):
                def store(cc, rt):
                    n = 4 * cc + rt
                    sy.wait_ge(mV["h"], f_eout(cc, rt))
                    estcnt[n] = dma(sy, qST[n % 2], eout_b[512 * cc + 128 * rt:512 * cc + 128 * (rt + 1), :],
                                    EOB[:, rt % 2, :])
                sy.wait_ge(qPC[0]["h"], wpre[0])
                sy.wait_ge(qPC[1]["h"], wpre[1])
                for c in range(NCH):
                    for m in range(MHID):
                        if 32 * c + m >= 4:
                            pc, pm = (c, m - 4) if m >= 4 else (c - 1, m + 28)
                            sy.wait_ge(mT["h"], f_mm2(pc, pm))
                        w2cnt[(c, m)] = dma(sy, qW2[m % 2], W2S[:, m % 4, :], w2bf[128 * m:128 * (m + 1), :])
                        if c >= 1 and m < 4:
                            store(c - 1, m)
                for rt in range(4):
                    store(NCH - 1, rt)
                sy.wait_ge(qST[0]["h"], qST[0]["n"])
                sy.wait_ge(qST[1]["h"], qST[1]["n"])

            @blk.tensor
            def _(tn):
                tn.wait_ge(mV["h"], b2b_V)
                tn.wait_ge(mG["h"], F_mg0 + 1)
                tn.wait_ge(qGA[0]["h"], xg_c[0])
                emit_xtr_mms(tn, 0, 0)
                tn.wait_ge(mV["h"], f_xtrAv(0))
                emit_xtr_mms(tn, 0, 1)
                for c in range(NCH):
                    for m in range(MHID):
                        ps = [psPK[:, 0:512], psPK[:, 512:1024], psV[:, 0:512], psV[:, 512:1024]][m % 4]
                        tn.wait_ge(qWQ[m % 2]["h"], w1_c[(c, m)])
                        if m == 0:
                            tn.wait_ge(mV["h"], f_xtrBv(c))
                        gi = 32 * c + m
                        if gi >= 4:
                            pc, pm = (c, m - 4) if m >= 4 else (c - 1, m + 28)
                            tn.wait_ge(mS["h"], f_gelu(pc, pm))
                        if m == 0 and c > 0:
                            tn.wait_ge(mV["h"], f_eout(c - 1, 3))
                        mm = None
                        for kc in range(DC):
                            mm = tn.matmul(ps, W1S[:, m % 4, kc, :], XG[:, kc, :],
                                           start=(kc == 0), stop=(kc == DC - 1))
                        bump(mT, mm)
                    if c + 1 < NCH:
                        tn.wait_ge(mG["h"], F_mg0 + c + 2)
                        tn.wait_ge(qGA[(c + 1) % 2]["h"], xg_c[c + 1])
                        if c >= 1:
                            tn.wait_ge(mV["h"], f_eout(c - 1, 1))
                        emit_xtr_mms(tn, c + 1, 0)
                        tn.wait_ge(mV["h"], f_xtrAv(c + 1))
                        emit_xtr_mms(tn, c + 1, 1)
                    else:
                        bump(mT, tn.drain())
                        bump(mT, tn.drain())
                    for m in range(MHID):
                        tn.wait_ge(qW2[m % 2]["h"], w2cnt[(c, m)])
                        if m == 0:
                            tn.wait_ge(mS["h"], F_s0 + 32 * (c + 1))
                            tn.wait_ge(mV["h"], f_xtrBv(c + 1) if c + 1 < NCH else mV["n"])
                            if c > 0:
                                tn.wait_ge(mV["h"], f_eout(c - 1, 3))
                        mm = None
                        for rt in range(4):
                            pst = [psA, psA, psPK, psV][rt]
                            base = [0, 1024, 0, 0][rt]
                            lhsT = H1S[:, m, :].rearrange("q (rt x) -> q rt x", rt=4, x=128)[:, rt, :]
                            for h2 in range(2):
                                mm = tn.matmul(pst[:, base + 512 * h2:base + 512 * (h2 + 1)],
                                               lhsT, W2S[:, m % 4, 512 * h2:512 * (h2 + 1)],
                                               start=(m == 0), stop=(m == MHID - 1))
                        bump(mT, mm)

            @blk.scalar
            def _(sc):
                for c in range(NCH):
                    for m in range(MHID):
                        sc.wait_ge(mT["h"], f_mm1(c, m))
                        if c > 0:
                            sc.wait_ge(mT["h"], f_mm2(c - 1, m))
                        ps = [psPK[:, 0:512], psPK[:, 512:1024], psV[:, 0:512], psV[:, 512:1024]][m % 4]
                        bump(mS, sc.activation(H1S[:, m, :], ps, ACT.Gelu, bias=B1S[:, m:m + 1]))

            @blk.vector
            def _(v):
                v.wait_ge(mT["h"], f_xtrA(0))
                emit_xtr_evac(v, 0, 0)
                v.wait_ge(mT["h"], f_xtrB(0))
                emit_xtr_evac(v, 0, 1)
                for c in range(NCH):
                    if c + 1 < NCH:
                        v.wait_ge(mT["h"], f_xtrA(c + 1))
                        emit_xtr_evac(v, c + 1, 0)
                        v.wait_ge(mT["h"], f_xtrB(c + 1))
                        emit_xtr_evac(v, c + 1, 1)
                    else:
                        bump(mV, v.drain())
                        bump(mV, v.drain())
                    v.wait_ge(mT["h"], f_mm2(c, 31))
                    for rt in range(4):
                        if 4 * c + rt >= 2:
                            v.wait_ge(qST[(4 * c + rt) % 2]["h"], estcnt[4 * c + rt - 2])
                        ps2 = [psA[:, 0:1024], psA[:, 1024:2048], psPK[:, :], psV[:, :]][rt]
                        bump(mV, v.tensor_add(EOB[:, rt % 2, :], ps2, BCB[:, 0, :]))
        f_eout_done = (qST[0]["n"], qST[1]["n"])

        if _STAGE == "eout":
            eodbg = nc.dram_tensor("eodbg", [CAP, D], BF16, kind="ExternalOutput")
            xgdbg = nc.dram_tensor("xgdbg", [128, 8192], BF16, kind="ExternalOutput")
            h1dbg = nc.dram_tensor("h1dbg", [128, MHID, RCH], BF16, kind="ExternalOutput")
            halldbg = nc.dram_tensor("halldbg", [NCORES * TCORE, D], BF16, kind="ExternalOutput")
            tabdbg = nc.dram_tensor("tabdbg", [ESTR], F32, kind="ExternalOutput")
            tabidbg = nc.dram_tensor("tabidbg", [NEXP * ESTR], F32, kind="ExternalOutput")
            epwdbg = nc.dram_tensor("epwdbg", [128, 4, NT], F32, kind="ExternalOutput")
            oi1dbg = nc.dram_tensor("oi1dbg", [128, NT], I32, kind="ExternalOutput")
            oi2dbg = nc.dram_tensor("oi2dbg", [128, NT], I32, kind="ExternalOutput")
            cntdbg = nc.dram_tensor("cntdbg", [128, NEXP], F32, kind="ExternalOutput")
            posdbg = nc.dram_tensor("posdbg", [128, NT, NEXP], F32, kind="ExternalOutput")
            mbsdbg = nc.dram_tensor("mbsdbg", [16, NEXP], F32, kind="ExternalOutput")
            mskdbg = nc.dram_tensor("mskdbg", [128, NT, NEXP], F32, kind="ExternalOutput")
            ltmdbg = nc.dram_tensor("ltmdbg", [128, NT, NEXP], F32, kind="ExternalOutput")
            gtsdbg = nc.dram_tensor("gtsdbg", [128, NT, NEXP], F32, kind="ExternalOutput")
            with nc.Block() as blk:
                @blk.sync
                def _(sy):
                    sy.wait_ge(qST[0]["h"], f_eout_done[0])
                    sy.wait_ge(qST[1]["h"], f_eout_done[1])
                    dma(sy, qMS, eodbg[:, :], eout_b[:, :])
                    dma(sy, qMS, xgdbg[:, :], XGS[:, :])
                    dma(sy, qMS, h1dbg[:, :, :], H1S[:, :, :])
                    dma(sy, qMS, halldbg[:, :], hall[:, :])
                    dma(sy, qMS, tabdbg.ap(), tab_o.ap().rearrange("(a) x -> (a x)"))
                    dma(sy, qMS, tabidbg.ap(), tab_i.ap().rearrange("(a) x -> (a x)"))
                    dma(sy, qMS, epwdbg[:, :, :], EPW[:, :, :])
                    dma(sy, qMS, oi1dbg[:, :], OI1[:, :])
                    dma(sy, qMS, oi2dbg[:, :], OI2[:, :])
                    dma(sy, qMS, cntdbg[:, :], cnt_o[:, :])
                    dma(sy, qMS, posdbg[:, :, :], POS[:, :, :])
                    dma(sy, qMS, mbsdbg[:, :], MBS[:, :])
                    dma(sy, qMS, mskdbg[:, :, :], MSK[:, :, :])
                    dma(sy, qMS, ltmdbg[:, :, :], LTM[:, :, :])
                    dma(sy, qMS, gtsdbg[:, :, :], GTS[:, :, :])
                    sy.wait_ge(qMS["h"], qMS["n"])
            return nc

        with nc.Block() as blk:
            @blk.gpsimd
            def _(g):
                for c in range(NCH):
                    g.wait_ge(qST[0]["h"], estcnt[4 * c + 2])
                    g.wait_ge(qST[1]["h"], estcnt[4 * c + 3])
                    g.collective_compute(
                        "AllGather", AL.bypass,
                        replica_groups=[list(range(NCORES))],
                        ins=[eout_b[512 * c:512 * (c + 1), :].opt()],
                        outs=[eout_all[4096 * c:4096 * (c + 1), :].opt()],
                    ).then_inc(sCC["h"], 1)
                    sCC["n"] += 1
            eall_cc = sCC["n"]

        G_v0 = mV["n"]
        G_mg0 = mG["n"]
        with nc.Block() as blk:
            @blk.gpsimd
            def _(g):
                g.wait_ge(sCC["h"], eall_cc)
                g.wait_ge(mV["h"], yi16_V)
                g.wait_ge(qMS["h"], yi16_dup)
                for q in range(4):
                    qq = qGA[0]
                    if qq["n"] > 0:
                        g.wait_ge(qq["h"], qq["n"])
                    g.dma_gather(G1B[:, 4 * q:4 * (q + 1), :], eout_all.ap(), YI16A[:, 32 * q:32 * (q + 1)],
                                 num_idxs=512, num_idxs_reg=512, elem_size=D).then_inc(qq["h"], 16)
                    qq["n"] += 16
                    qq2 = qGA[1]
                    if qq2["n"] > 0:
                        g.wait_ge(qq2["h"], qq2["n"])
                    g.dma_gather(G2B[:, 4 * q:4 * (q + 1), :], eout_all.ap(), YI16B[:, 32 * q:32 * (q + 1)],
                                 num_idxs=512, num_idxs_reg=512, elem_size=D).then_inc(qq2["h"], 16)
                    qq2["n"] += 16
            g1_dma = qGA[0]["n"]
            g2_dma = qGA[1]["n"]

            gld_cnt = {}
            gst_cnt = {}

            @blk.sync
            def _(sy):
                def store(j):
                    sy.wait_ge(mV["h"], G_v0 + j + 1)
                    gst_cnt[j] = dma(sy, qST[j % 2], out[128 * j:128 * (j + 1), :], XNEW[:, j % 2, :])
                for t in range(NT):
                    if t >= 2:
                        sy.wait_ge(mV["h"], G_v0 + (t - 2) + 1)
                    gld_cnt[t] = dma(sy, qXL[t % 2], XSTR[:, t % 2, :], x2d[128 * t:128 * (t + 1), :])
                    if t >= 2:
                        store(t - 2)
                store(NT - 2)
                store(NT - 1)
                sy.wait_ge(qST[0]["h"], qST[0]["n"])
                sy.wait_ge(qST[1]["h"], qST[1]["n"])

            @blk.vector
            def _(v):
                for t in range(NT):
                    v.wait_ge(qGA[0]["h"], g1_dma - 64 + 16 * (t // 4 + 1))
                    v.wait_ge(qGA[1]["h"], g2_dma - 64 + 16 * (t // 4 + 1))
                    v.wait_ge(qXL[t % 2]["h"], gld_cnt[t])
                    if t >= 2:
                        v.wait_ge(qST[t % 2]["h"], gst_cnt[t - 2])
                    v.scalar_tensor_tensor(XNEW[:, t % 2, :], G1B[:, t, :], WCOL[:, 0, t:t + 1],
                                           XSTR[:, t % 2, :], AL.mult, AL.add)
                    v.drain()
                    bump(mV, v.scalar_tensor_tensor(XNEW[:, t % 2, :], G2B[:, t, :], WCOL[:, 1, t:t + 1],
                                                    XNEW[:, t % 2, :], AL.mult, AL.add))

        return nc


def _host_consts(core):
    tri = np.triu(np.ones((128, 128), np.float32), 1)
    selm = np.zeros((128, 16), np.float32)
    for i in range(16):
        selm[16 * core + i, i] = 1.0
    iota8 = np.broadcast_to(np.arange(8, dtype=np.float32), (128, 8)).copy()
    tokid = np.zeros((128, NT), np.float32)
    for t in range(NT):
        for p in range(128):
            tokid[p, t] = core * TCORE + t * 128 + p
    onesr = np.ones((1, 128), np.float32)
    onec = np.ones((128, 1), np.float32)
    id8 = np.eye(8, dtype=np.float32)
    tsel = np.zeros((16, 2048), np.float32)
    for t in range(16):
        tsel[t, 128 * t:128 * (t + 1)] = 1.0
    import ml_dtypes
    idb = np.eye(128, dtype=ml_dtypes.bfloat16)
    return dict(tri=tri, selm=selm, iota8=iota8, tokid=tokid, onesr=onesr,
                onec=onec, id8=id8, tsel=tsel, idb=idb)


def kernel(L0_attn_x, L0_attn_W_qkv, L0_attn_RF, L0_attn_W_out, L0_attn_b_out,
           L0_ffn_router_w, L0_ffn_router_b, L0_ffn_w1, L0_ffn_b1, L0_ffn_w2, L0_ffn_b2):
    x = np.ascontiguousarray(np.asarray(L0_attn_x, dtype=np.float32)).reshape(NCORES * TCORE, D)
    wq0 = np.asarray(L0_attn_W_qkv, dtype=np.float32)
    wqb = wq0[:, 0:D].reshape(DC, 128, NPAIR, 128).transpose(2, 1, 0, 3).reshape(D, D)
    wkb = wq0[:, D:2 * D].reshape(DC, 128, NPAIR, 128).transpose(2, 1, 0, 3).reshape(D, D)
    rww0 = np.asarray(L0_ffn_router_w, dtype=np.float32)
    args = dict(
        wqkv=np.ascontiguousarray(np.stack([wqb, wkb, wq0[:, 2 * D:3 * D]], axis=0)),
        rfin=np.ascontiguousarray(np.asarray(L0_attn_RF, dtype=np.float32)),
        wout=np.ascontiguousarray(np.asarray(L0_attn_W_out, dtype=np.float32)),
        bout=np.ascontiguousarray(np.asarray(L0_attn_b_out, dtype=np.float32)),
        rww=np.ascontiguousarray(rww0.reshape(DC, 128, NEXP).transpose(1, 0, 2).reshape(128, DC * NEXP)),
        rwb=np.ascontiguousarray(np.asarray(L0_ffn_router_b, dtype=np.float32)),
    )
    w1 = np.ascontiguousarray(np.asarray(L0_ffn_w1, dtype=np.float32))
    b1 = np.ascontiguousarray(np.asarray(L0_ffn_b1, dtype=np.float32))
    w2 = np.ascontiguousarray(np.asarray(L0_ffn_w2, dtype=np.float32))
    b2 = np.ascontiguousarray(np.asarray(L0_ffn_b2, dtype=np.float32))

    nc = build_nc()
    mybir.codegen_inst_isa_subclasses(nc)
    in_maps = []
    for c in range(NCORES):
        m = dict(args)
        m["xin"] = x[c * TCORE:(c + 1) * TCORE]
        m["w1e"] = np.ascontiguousarray(
            w1[c].reshape(DC, 128, MHID, 128).transpose(2, 1, 0, 3).reshape(MHID * 128, DC * 128))
        m["b1e"] = np.ascontiguousarray(b1[c].reshape(MHID, 128).T)
        m["w2e"] = w2[c]
        m["b2e"] = b2[c]
        m.update(_host_consts(c))
        in_maps.append(m)
    res = run_bass_kernel_spmd(nc, in_maps, core_ids=list(range(NCORES)))
    y = np.concatenate([np.asarray(r["out"]) for r in res.results], axis=0)
    return y.reshape(2, 8192, 1024)


# version 16 (final)
# speedup vs baseline: 1.0633x; 1.0633x over previous
import contextlib

import numpy as np

import concourse.bass as bass
import concourse.mybir as mybir
from concourse.bass import IndirectOffsetOnAxis
from concourse.bass_utils import run_bass_kernel_spmd
from concourse.library_config import mlp as _mlp_lib

F32 = mybir.dt.float32
BF16 = mybir.dt.bfloat16
I16 = mybir.dt.int16
I32 = mybir.dt.int32
ACT = mybir.ActivationFunctionType
AL = mybir.AluOpType
AXX = mybir.AxisListType.X

NCORES = 8
TCORE = 2048
NT = 16
D = 1024
DC = 8
NPAIR = 8
DHID = 4096
MHID = 32
NEXP = 8
CAP = 2560
ESTR = 2688
ILC = 168
RCH = 512
NCH = 5
GQ = 128
NGQ = TCORE // GQ
EPS_LN = 1e-5
STAB = 1e-6

_STAGE = "full"


def build_nc():
    nc = bass.Bass("TRN2", target_bir_lowering=False)

    xin = nc.dram_tensor("xin", [TCORE, D], F32, kind="ExternalInput")
    wqkv = nc.dram_tensor("wqkv", [3, D, D], F32, kind="ExternalInput")
    rfin = nc.dram_tensor("rfin", [16, 64, 64], F32, kind="ExternalInput")
    wout = nc.dram_tensor("wout", [D, D], F32, kind="ExternalInput")
    bout = nc.dram_tensor("bout", [D], F32, kind="ExternalInput")
    rww = nc.dram_tensor("rww", [128, DC * NEXP], F32, kind="ExternalInput")
    rwb = nc.dram_tensor("rwb", [NEXP], F32, kind="ExternalInput")
    w1e = nc.dram_tensor("w1e", [MHID * 128, DC * 128], F32, kind="ExternalInput")
    b1e = nc.dram_tensor("b1e", [128, MHID], F32, kind="ExternalInput")
    w2e = nc.dram_tensor("w2e", [DHID, D], F32, kind="ExternalInput")
    b2e = nc.dram_tensor("b2e", [D], F32, kind="ExternalInput")
    tri = nc.dram_tensor("tri", [128, 128], F32, kind="ExternalInput")
    selm = nc.dram_tensor("selm", [128, 16], F32, kind="ExternalInput")
    iota8 = nc.dram_tensor("iota8", [128, NEXP], F32, kind="ExternalInput")
    tokid = nc.dram_tensor("tokid", [128, NT], F32, kind="ExternalInput")
    onesr = nc.dram_tensor("onesr", [1, 128], F32, kind="ExternalInput")
    onec = nc.dram_tensor("onec", [128, 1], F32, kind="ExternalInput")
    id8 = nc.dram_tensor("id8", [8, 8], F32, kind="ExternalInput")
    tsel = nc.dram_tensor("tsel", [16, 2048], F32, kind="ExternalInput")
    idb = nc.dram_tensor("idb", [128, 128], BF16, kind="ExternalInput")

    out = nc.dram_tensor("out", [TCORE, D], F32, kind="ExternalOutput")

    x2d = nc.dram_tensor("x2d", [TCORE, D], F32)
    hb = nc.dram_tensor("hb", [TCORE, D], BF16)
    hall = nc.dram_tensor("hall", [NCORES * TCORE, D], BF16, addr_space="Shared")
    kvb_i = nc.dram_tensor("kvb_i", [128, 520], F32)
    kvb_o = nc.dram_tensor("kvb_o", [128, 520], F32)
    cnt_i = nc.dram_tensor("cnt_i", [16, NEXP], F32)
    cnt_o = nc.dram_tensor("cnt_o", [128, NEXP], F32, addr_space="Shared")
    tab_i = nc.dram_tensor("tab_i", [NEXP * ESTR, 1], F32)
    tab_o = nc.dram_tensor("tab_o", [ESTR, 1], F32)
    yd1 = nc.dram_tensor("yd1", [128 * NT], F32)
    yd2 = nc.dram_tensor("yd2", [128 * NT], F32)
    w2bf = nc.dram_tensor("w2bf", [DHID, D], BF16)
    w1bf = nc.dram_tensor("w1bf", [MHID * 128, DC * 128], BF16)
    eout_b = nc.dram_tensor("eout_b", [CAP, D], BF16)
    eout_all = nc.dram_tensor("eout_all", [NEXP * CAP, D], BF16, addr_space="Shared")

    ctx = contextlib.ExitStack()
    with ctx:
        AB1 = ctx.enter_context(nc.sbuf_tensor("AB1", [128, 40960], BF16))
        AB2 = ctx.enter_context(nc.sbuf_tensor("AB2", [128, 32768], BF16))
        AFM = ctx.enter_context(nc.sbuf_tensor("AFM", [128, 10256], F32))
        XGS = ctx.enter_context(nc.sbuf_tensor("XGS", [128, 8192], BF16))

        def view(t, off, shape):
            n = int(np.prod(shape))
            ap = t[:, off:off + n]
            if len(shape) == 1:
                return ap
            if len(shape) == 2:
                return ap.rearrange("p (a b) -> p a b", a=shape[0], b=shape[1])
            return ap.rearrange("p (a b c) -> p a b c", a=shape[0], b=shape[1], c=shape[2])

        FMA = view(AB1, 0, [DC, 2048])
        H1S = view(AB1, 0, [MHID, RCH])
        EOB = view(AB1, 16384, [2, 1024])
        G1B = view(AB1, 0, [NT, 1024])
        G2B = view(AB2, 0, [NT, 1024])
        TMA = view(AB1, 16384, [DC, NT, 128])
        WRF1 = view(AB1, 32768, [DC, 1024])
        WOS = WRF1
        PHQ = view(AB2, 0, [NPAIR, 2048])
        HLOT = view(AB2, 0, [DC, NT, 128])
        WRF2 = view(AB2, 16384, [DC, 1024])
        WVS = view(AB2, 24576, [DC, 1024])
        HLOF = view(AB2, 16384, [DC, 2048])
        WSLAB = view(XGS, 0, [2, 2, 1024])
        WTSLB = view(XGS, 4096, [2, 2, 1024])
        PKS = view(XGS, 0, [2, 1024])
        VAS = view(XGS, 2048, [2, 1040])
        W2C = view(XGS, 0, [4, 1024])
        XG = view(XGS, 0, [8, RCH])
        XTM = view(XGS, 4096, [4, 1024])

        XSTR = view(AFM, 0, [2, 1024])
        XNEW = view(AFM, 2048, [2, 1024])
        BCB = view(AFM, 4096, [1, 1024])
        KVCAT = view(AFM, 5120, [1, 520])
        KVAR = view(AFM, 5640, [1, 520])
        LGT = view(AFM, 6160, [1, 2048])
        KVACC = view(AFM, 6160, [8, 130])
        G1Q = view(AFM, 6160, [2, 1, 1024])
        G2Q = view(AFM, 8208, [2, 1, 1024])

        TSEL = AFM[0:16, 0:2048]
        BROW = AFM[0:1, 5120:6144]
        KVAUG = ctx.enter_context(nc.sbuf_tensor("KVAUG", [128, NPAIR, 130], BF16))
        W1S = view(AB2, 16384, [4, 8, 128])
        W2S = view(AB2, 24576, [4, 1024])
        RFBD = view(AB2, 0, [NPAIR, 128])
        STATS = ctx.enter_context(nc.sbuf_tensor("STATS", [128, 12], F32))
        MRS = ctx.enter_context(nc.sbuf_tensor("MRS", [128, NT, 4], F32))
        LTM = view(AFM, 2728, [NT, 8])
        MSK = ctx.enter_context(nc.sbuf_tensor("MSK", [128, NT, 8], F32))
        GTS = ctx.enter_context(nc.sbuf_tensor("GTS", [128, NT, 8], F32))
        POS = ctx.enter_context(nc.sbuf_tensor("POS", [128, NT, 8], F32))
        WGT = view(AFM, 2856, [NT, 8])
        E8T = view(AFM, 2472, [NT, 8])
        M12 = ctx.enter_context(nc.sbuf_tensor("M12", [128, NT, 2], F32))
        RTS = ctx.enter_context(nc.sbuf_tensor("RTS", [128, NT, 3], F32))
        SC8 = ctx.enter_context(nc.sbuf_tensor("SC8", [128, 8], F32))
        SC8B = ctx.enter_context(nc.sbuf_tensor("SC8B", [128, 8], F32))
        SCC = ctx.enter_context(nc.sbuf_tensor("SCC", [128, 16], F32))
        EPW = ctx.enter_context(nc.sbuf_tensor("EPW", [128, 4, NT], F32))
        WCOL = ctx.enter_context(nc.sbuf_tensor("WCOL", [128, 2, NT], F32))
        YOF = ctx.enter_context(nc.sbuf_tensor("YOF", [128, 2, NT], F32))
        OI1 = ctx.enter_context(nc.sbuf_tensor("OI1", [128, NT], I32))
        OIS = ctx.enter_context(nc.sbuf_tensor("OIS", [128, 4], I32))
        OI2 = ctx.enter_context(nc.sbuf_tensor("OI2", [128, NT], I32))
        TOKF = ctx.enter_context(nc.sbuf_tensor("TOKF", [128, NT], F32))
        CNTT = ctx.enter_context(nc.sbuf_tensor("CNTT", [8, 16], F32))
        CNTS = ctx.enter_context(nc.sbuf_tensor("CNTS", [16, 8], F32))
        CAG = ctx.enter_context(nc.sbuf_tensor("CAG", [128, 8], F32))
        GBS = ctx.enter_context(nc.sbuf_tensor("GBS", [128, 8], F32))
        MBS = ctx.enter_context(nc.sbuf_tensor("MBS", [16, 8], F32))
        TRIS = view(AFM, 2984, [1, 128]).rearrange("p a b -> p (a b)")
        SELS = ctx.enter_context(nc.sbuf_tensor("SELS", [128, 16], F32))
        IOTS = ctx.enter_context(nc.sbuf_tensor("IOTS", [128, 8], F32))
        ONER = ctx.enter_context(nc.sbuf_tensor("ONER", [1, 128], F32))
        ONEC = ctx.enter_context(nc.sbuf_tensor("ONEC", [128, 1], F32))
        ID8S = ctx.enter_context(nc.sbuf_tensor("ID8S", [8, 8], F32))
        RWBS = ctx.enter_context(nc.sbuf_tensor("RWBS", [8, 1], F32))
        RWF = ctx.enter_context(nc.sbuf_tensor("RWF", [128, 8, 8], F32))
        IDB = ctx.enter_context(nc.sbuf_tensor("IDB", [128, 128], BF16))
        RWHI = ctx.enter_context(nc.sbuf_tensor("RWHI", [128, 8, 8], BF16))
        RWLO = ctx.enter_context(nc.sbuf_tensor("RWLO", [128, 8, 8], BF16))
        B1S = ctx.enter_context(nc.sbuf_tensor("B1S", [128, MHID], F32))
        XI16 = ctx.enter_context(nc.sbuf_tensor("XI16", [128, ILC], I16))
        XIF = AFM[0:16, 2304:2472]
        YI16A = ctx.enter_context(nc.sbuf_tensor("YI16A", [128, 128], I16))
        YI16B = ctx.enter_context(nc.sbuf_tensor("YI16B", [128, 128], I16))
        YIFA = AFM[0:16, 2048:2176]
        YIFB = AFM[0:16, 2176:2304]
        ZTAB = view(XGS, 4096, [1, ILC])

        psA = ctx.enter_context(nc.psum_tensor("psA", [128, 2048], F32))
        psPK = ctx.enter_context(nc.psum_tensor("psPK", [128, 1024], F32))
        psV = ctx.enter_context(nc.psum_tensor("psV", [128, 1024], F32))

        def mkq(name):
            return {"h": ctx.enter_context(nc.semaphore(name)), "n": 0}

        qXL = [mkq("qXL0"), mkq("qXL1")]
        qSP = [mkq("qSP0"), mkq("qSP1")]
        qST = [mkq("qST0"), mkq("qST1")]
        qPC = [mkq("qPC0"), mkq("qPC1")]
        qMS = mkq("qMS")
        qW2 = [mkq("qW20"), mkq("qW21")]
        qTR = [mkq("qTR0"), mkq("qTR1")]
        qWQ = [mkq("qWQ0"), mkq("qWQ1")]
        qGA = [mkq("qGA0"), mkq("qGA1")]
        qGM = mkq("qGM")
        qSC = mkq("qSC")
        sCC = {"h": ctx.enter_context(nc.semaphore("sCC")), "n": 0}
        mT = {"h": ctx.enter_context(nc.semaphore("mT")), "n": 0}
        mV = {"h": ctx.enter_context(nc.semaphore("mV")), "n": 0}
        mS = {"h": ctx.enter_context(nc.semaphore("mS")), "n": 0}
        mG = {"h": ctx.enter_context(nc.semaphore("mG")), "n": 0}

        def dma(eng, q, out_, in_, prewait=True, **kw):
            if prewait and q["n"] > 0:
                eng.wait_ge(q["h"], q["n"])
            eng.dma_start(out=out_, in_=in_, **kw).then_inc(q["h"], 16)
            q["n"] += 16
            return q["n"]

        def bump(sem, inst, k=1):
            inst.then_inc(sem["h"], k)
            sem["n"] += k
            return sem["n"]

        with nc.Block() as blk:
            @blk.sync
            def _(sy):
                dma(sy, qMS, SELS[:, :], selm[:, :])
                dma(sy, qMS, IOTS[:, :], iota8[:, :])
                dma(sy, qMS, TOKF[:, :], tokid[:, :])
                dma(sy, qMS, ONER[:, :], onesr[:, :])
                dma(sy, qMS, ONEC[:, :], onec[:, :])
                dma(sy, qMS, ID8S[:, :], id8[:, :])
                dma(sy, qMS, IDB[:, :], idb[:, :])
                dma(sy, qMS, RWBS[:, :], rwb.ap().rearrange("(e x) -> e x", x=1))
                dma(sy, qMS, RWF[:, :, :], rww.ap().rearrange("q (kc e) -> q kc e", kc=DC, e=NEXP))
                dma(sy, qMS, B1S[:, :], b1e.ap())
        consts_ld = qMS["n"]

        sA0_v0 = mV["n"]
        sA0_s0 = mS["n"]
        with nc.Block() as blk:
            xac = {}

            @blk.sync
            def _(sy):
                for t in range(NT):
                    if t >= 2:
                        sy.wait_ge(mS["h"], sA0_s0 + (t - 1))
                    xac[t] = dma(sy, qXL[t % 2], XSTR[:, t % 2, :], xin[128 * t:128 * (t + 1), :])

            @blk.vector
            def _(v):
                for t in range(NT):
                    v.wait_ge(qXL[t % 2]["h"], xac[t])
                    xt = XSTR[:, t % 2, :]
                    v.bn_stats(STATS[:, 0:6], xt[:, 0:512])
                    v.bn_stats(STATS[:, 6:12], xt[:, 512:1024])
                    v.drain()
                    v.bn_aggr(MRS[:, t, 0:2], STATS[:, :])
                    v.drain()
                    v.tensor_scalar(MRS[:, t, 2:3], MRS[:, t, 1:2], 1.0, EPS_LN, AL.mult, AL.add)
                    v.drain()
                    v.reciprocal(MRS[:, t, 3:4], MRS[:, t, 2:3])
                    v.drain()
                    bump(mV, v.drain())

            @blk.scalar
            def _(sc):
                for t in range(NT):
                    sc.wait_ge(mV["h"], sA0_v0 + t + 1)
                    sc.activation(MRS[:, t, 2:3], MRS[:, t, 3:4], ACT.Sqrt)
                    sc.drain()
                    sc.mul(MRS[:, t, 3:4], MRS[:, t, 0:1], MRS[:, t, 2:3])
                    sc.drain()
                    sc.mul(MRS[:, t, 3:4], MRS[:, t, 3:4], -1.0)
                    sc.drain()
                    bump(mS, sc.activation(TMA[:, :, t, :], XSTR[:, t % 2, :], ACT.Identity,
                                           bias=MRS[:, t, 3:4], scale=MRS[:, t, 2:3]))
        ln1_S = mS["n"]

        sA1_t0 = mT["n"]
        sA1_v0 = mV["n"]
        a1T = {}
        a1V = {}
        for _p in range(NPAIR):
            for _qk in range(2):
                a1T[("st", _p, _qk)] = sA1_t0 + 4 * _p + _qk + 1
                a1V[("st", _p, _qk)] = sA1_v0 + 4 * _p + _qk + 1
                a1T[("wrf", _p, _qk)] = sA1_t0 + 4 * _p + 2 + _qk + 1
                a1V[("wrf", _p, _qk)] = sA1_v0 + 4 * _p + 2 + _qk + 1
        for _t in range(NT):
            a1T[("lt", _t)] = sA1_t0 + 32 + _t + 1
            a1V[("lt", _t)] = sA1_v0 + 32 + _t + 1
        with nc.Block() as blk:
            @blk.gpsimd
            def _(g):
                g.load_library(_mlp_lib)
                g.memset(RFBD[:, :, :], 0.0)
                g.drain()
                for p in range(NPAIR):
                    dma(g, qGM, RFBD[0:64, p, 0:64], rfin[2 * p, :, :])
                    dma(g, qGM, RFBD[64:128, p, 64:128], rfin[2 * p + 1, :, :])
            rf_ld = qGM["n"]

            wq_c = {}

            @blk.gpsimd
            def _(g):
                for p in range(NPAIR):
                    if p >= 2:
                        g.wait_ge(mT["h"], a1T[("st", p - 1, 1)])
                    dma(g, qWQ[p % 2], WSLAB[:, p % 2, 0, :], wqkv.ap()[0, 128 * p:128 * (p + 1), :])
                    wq_c[p] = dma(g, qWQ[p % 2], WSLAB[:, p % 2, 1, :], wqkv.ap()[1, 128 * p:128 * (p + 1), :])

            def unit_ps(u):
                return psPK if (u % 2 == 0) else psV

            ukey = {}
            for p in range(NPAIR):
                ukey[2 * p] = ("st", p, 0)
                ukey[2 * p + 1] = ("st", p, 1)
            for t in range(NT):
                ukey[16 + t] = ("lt", t)

            @blk.tensor
            def _(tn):
                tn.wait_ge(qGM["h"], rf_ld)
                tn.wait_ge(qMS["h"], consts_ld)
                for p in range(NPAIR):
                    for qk in range(2):
                        u = 2 * p + qk
                        if qk == 0:
                            tn.wait_ge(qWQ[p % 2]["h"], wq_c[p])
                        if u >= 2:
                            tn.wait_ge(mV["h"], a1V[ukey[u - 2]])
                        wsl = WSLAB[:, p % 2, qk, :].rearrange("q (kc m) -> q kc m", kc=DC, m=128)
                        ps = unit_ps(u)
                        mm = None
                        for kc in range(DC):
                            mm = tn.matmul(ps[:, 128 * kc:128 * (kc + 1)], wsl[:, kc, :], IDB[:, :],
                                           start=True, stop=True)
                        a1T[("st", p, qk)] = bump(mT, mm)
                    for qk in range(2):
                        tn.wait_ge(mV["h"], a1V[("st", p, qk)])
                        if p >= 1:
                            tn.wait_ge(mV["h"], a1V[("wrf", p - 1, qk)])
                        wt = WTSLB[:, p % 2, qk, :].rearrange("dh (kc dd) -> dh kc dd", kc=DC, dd=128)
                        mm = None
                        for dc in range(DC):
                            mm = tn.matmul(psA[:, 1024 * qk + 128 * dc:1024 * qk + 128 * (dc + 1)],
                                           wt[:, dc, :], RFBD[:, p, :], start=True, stop=True)
                        a1T[("wrf", p, qk)] = bump(mT, mm)
                for t in range(NT):
                    u = 16 + t
                    tn.wait_ge(mS["h"], sA0_s0 + t + 1)
                    tn.wait_ge(mV["h"], a1V[ukey[u - 2]])
                    ps = unit_ps(u)
                    mm = None
                    for c in range(DC):
                        mm = tn.matmul(ps[:, 128 * c:128 * (c + 1)], TMA[:, c, t, :], IDB[:, :],
                                       start=True, stop=True)
                    a1T[("lt", t)] = bump(mT, mm)

            @blk.vector
            def _(v):
                w1v = WRF1.rearrange("q dc (a b) -> q dc a b", a=8, b=128)
                w2v = WRF2.rearrange("q dc (a b) -> q dc a b", a=8, b=128)
                for p in range(NPAIR):
                    for qk in range(2):
                        v.wait_ge(mT["h"], a1T[("st", p, qk)])
                        wtd = WTSLB[:, p % 2, qk, :].rearrange("dh (kc dd) -> dh kc dd", kc=DC, dd=128)
                        ps = unit_ps(2 * p + qk)
                        mm = None
                        for kc in range(DC):
                            mm = v.tensor_copy(wtd[:, kc, :], ps[:, 128 * kc:128 * (kc + 1)])
                        a1V[("st", p, qk)] = bump(mV, mm)
                    for qk in range(2):
                        v.wait_ge(mT["h"], a1T[("wrf", p, qk)])
                        wv = w1v if qk == 0 else w2v
                        mm = None
                        for dc in range(DC):
                            mm = v.tensor_copy(wv[:, dc, p, :],
                                               psA[:, 1024 * qk + 128 * dc:1024 * qk + 128 * (dc + 1)])
                        a1V[("wrf", p, qk)] = bump(mV, mm)
                for t in range(NT):
                    v.wait_ge(mT["h"], a1T[("lt", t)])
                    ps = unit_ps(16 + t)
                    fmv = FMA[:, :, :].rearrange("f c (tt p) -> f c tt p", tt=NT, p=128)
                    mm = None
                    for c in range(DC):
                        mm = v.tensor_copy(fmv[:, c, t, :], ps[:, 128 * c:128 * (c + 1)])
                    a1V[("lt", t)] = bump(mV, mm)
        wrf_T = mT["n"]
        wrf_V = mV["n"]
        assert wrf_T == sA1_t0 + 48 and wrf_V == sA1_v0 + 48, (wrf_T, sA1_t0, wrf_V, sA1_v0)

        with nc.Block() as blk:
            @blk.gpsimd
            def _(g):
                for kc in range(DC):
                    dma(g, qGM, WVS[:, kc, :], wqkv.ap()[2, 128 * kc:128 * (kc + 1), :])
        wv_ld = qGM["n"]

        sA2_t0 = mT["n"]
        sA2_s0 = mS["n"]
        with nc.Block() as blk:
            @blk.tensor
            def _(tn):
                tn.wait_ge(mV["h"], wrf_V)
                w1v = WRF1.rearrange("q dc (a b) -> q dc a b", a=8, b=128)
                for p in range(NPAIR):
                    if p >= 1:
                        tn.wait_ge(mS["h"], sA2_s0 + p)
                    mm = None
                    for kc in range(DC):
                        for h4 in range(4):
                            mm = tn.matmul(psA[:, 512 * h4:512 * (h4 + 1)], w1v[:, kc, p, :],
                                           FMA[:, kc, 512 * h4:512 * (h4 + 1)],
                                           start=(kc == 0), stop=(kc == DC - 1))
                    bump(mT, mm)

            @blk.scalar
            def _(sc):
                for p in range(NPAIR):
                    sc.wait_ge(mT["h"], sA2_t0 + p + 1)
                    bump(mS, sc.activation(PHQ[:, p, :], psA[:, :], ACT.Relu))
        phiq_T = mT["n"]
        phiq_S = mS["n"]

        sA3_t0 = mT["n"]
        sA3_s0 = mS["n"]
        sA3_v0 = mV["n"]
        sA3_g0 = mG["n"]
        with nc.Block() as blk:
            @blk.gpsimd
            def _(g):
                g.wait_ge(mV["h"], wrf_V)
                for b in range(2):
                    vv = VAS[:, b, :].rearrange("q (a c) -> q a c", a=8, c=130)
                    g.memset(vv[:, :, 128:129], 1.0)
                    g.memset(vv[:, :, 129:130], 0.0)
                g.drain()
                bump(mG, g.drain())
                for j in range(8):
                    dma(g, qPC[j % 2], w2bf[512 * j:512 * (j + 1), :], w2e.ap()[512 * j:512 * (j + 1), :],
                        prewait=False)
                    dma(g, qPC[j % 2], w1bf[512 * j:512 * (j + 1), :], w1e.ap()[512 * j:512 * (j + 1), :],
                        prewait=False)

            @blk.tensor
            def _(tn):
                tn.wait_ge(mS["h"], phiq_S)
                tn.wait_ge(qGM["h"], wv_ld)
                tn.wait_ge(mG["h"], sA3_g0 + 1)
                for t in range(NT):
                    b = t % 2
                    if t >= 2:
                        tn.wait_ge(mS["h"], sA3_s0 + (t - 1))
                        tn.wait_ge(mV["h"], sA3_v0 + 2 * (t - 2) + 1)
                    mm = None
                    for kc in range(DC):
                        lhsT = FMA[:, kc, :].rearrange("f (t p) -> f t p", t=NT, p=128)[:, t, :]
                        for h2 in range(2):
                            tn.matmul(psPK[:, 512 * h2:512 * (h2 + 1)], lhsT,
                                      WRF2[:, kc, 512 * h2:512 * (h2 + 1)],
                                      start=(kc == 0), stop=(kc == DC - 1))
                            mm = tn.matmul(psV[:, 512 * h2:512 * (h2 + 1)], lhsT,
                                           WVS[:, kc, 512 * h2:512 * (h2 + 1)],
                                           start=(kc == 0), stop=(kc == DC - 1))
                    bump(mT, mm)
                    tn.wait_ge(mS["h"], sA3_s0 + t + 1)
                    tn.wait_ge(mV["h"], sA3_v0 + 2 * t + 1)
                    if t >= 1:
                        tn.wait_ge(mV["h"], sA3_v0 + 2 * (t - 1) + 2)
                    mm = None
                    for p in range(NPAIR):
                        mm = tn.matmul(psA[:, 256 * p:256 * p + 130],
                                       PKS[:, b, :].rearrange("q (a c) -> q a c", a=8, c=128)[:, p, :],
                                       VAS[:, b, :].rearrange("q (a c) -> q a c", a=8, c=130)[:, p, :],
                                       start=True, stop=True)
                    bump(mT, mm)

            @blk.scalar
            def _(sc):
                for t in range(NT):
                    sc.wait_ge(mT["h"], sA3_t0 + 2 * t + 1)
                    if t >= 2:
                        sc.wait_ge(mT["h"], sA3_t0 + 2 * (t - 1) + 2)
                    bump(mS, sc.activation(PKS[:, t % 2, :], psPK[:, :], ACT.Relu))

            @blk.vector
            def _(v):
                kvsrc = psA.ap().rearrange("q (a c) -> q a c", a=8, c=256)[:, :, 0:130]
                for t in range(NT):
                    v.wait_ge(mT["h"], sA3_t0 + 2 * t + 1)
                    if t >= 2:
                        v.wait_ge(mT["h"], sA3_t0 + 2 * (t - 1) + 2)
                    vv = VAS[:, t % 2, :].rearrange("q (a c) -> q a c", a=8, c=130)
                    v.tensor_copy(vv[:, :, 0:128], psV.ap().rearrange("q (a c) -> q a c", a=8, c=128))
                    v.drain()
                    bump(mV, v.drain())
                    v.wait_ge(mT["h"], sA3_t0 + 2 * t + 2)
                    if t == 0:
                        v.tensor_copy(KVACC[:, :, :], kvsrc)
                    else:
                        v.tensor_tensor(KVACC[:, :, :], KVACC[:, :, :], kvsrc, AL.add)
                    v.drain()
                    bump(mV, v.drain())
        tile_T = mT["n"]
        wpre = (qPC[0]["n"], qPC[1]["n"])

        with nc.Block() as blk:
            @blk.vector
            def _(v):
                v.wait_ge(qMS["h"], consts_ld)
                v.tensor_copy(RWHI[:, :, :], RWF[:, :, :])
                v.drain()
                mm = None
                for kc in range(DC):
                    mm = v.tensor_tensor(RWLO[:, kc, :], RWF[:, kc, :], RWHI[:, kc, :], AL.subtract)
                bump(mV, mm)
        rwc_V = mV["n"]

        with nc.Block() as blk:
            @blk.vector
            def _(v):
                kvv = KVACC
                kcat = KVCAT[:, 0, :].rearrange("q (a c) -> q a c", a=8, c=65)
                for p in range(NPAIR):
                    v.tensor_copy(kcat[0:64, p, 0:64], kvv[0:64, p, 0:64])
                    v.tensor_copy(kcat[0:64, p, 64:65], kvv[0:64, p, 128:129])
                    v.tensor_copy(kcat[64:128, p, 0:64], kvv[64:128, p, 64:128])
                    v.tensor_copy(kcat[64:128, p, 64:65], kvv[64:128, p, 128:129])
                v.drain()
                bump(mV, v.drain())
            kvpack_V = mV["n"]

            @blk.sync
            def _(sy):
                sy.wait_ge(mV["h"], kvpack_V)
                dma(sy, qMS, kvb_i[:, :], KVCAT[:, 0, :])
            kv_st = qMS["n"]

            @blk.gpsimd
            def _(g):
                g.wait_ge(qMS["h"], kv_st)
                g.collective_compute(
                    "AllReduce", AL.add,
                    replica_groups=[[0, 1, 2, 3], [4, 5, 6, 7]],
                    ins=[kvb_i.ap().opt()], outs=[kvb_o.ap().opt()],
                ).then_inc(sCC["h"], 1)
                sCC["n"] += 1
            kv_cc = sCC["n"]

            @blk.sync
            def _(sy):
                sy.wait_ge(sCC["h"], kv_cc)
                dma(sy, qMS, KVAR[:, 0, :], kvb_o[:, :])
            kv_ld = qMS["n"]

            @blk.vector
            def _(v):
                v.wait_ge(qMS["h"], kv_ld)
                v.memset(KVAUG[:, :, :], 0.0)
                v.drain()
                kar = KVAR[:, 0, :].rearrange("q (a c) -> q a c", a=8, c=65)
                for p in range(NPAIR):
                    v.tensor_copy(KVAUG[0:64, p, 0:65], kar[0:64, p, :])
                    v.tensor_copy(KVAUG[64:128, p, 65:130], kar[64:128, p, :])
                v.drain()
                bump(mV, v.drain())
        kvaug_V = mV["n"]

        sC1_t0 = mT["n"]
        sC1_v0 = mV["n"]

        def outps(t):
            if t % 2 == 0:
                return [psA[:, 256 * p:256 * p + 130] for p in range(8)]
            return [psPK[:, 256 * p:256 * p + 130] for p in range(4)] + \
                   [psV[:, 256 * (p - 4):256 * (p - 4) + 130] for p in range(4, 8)]

        with nc.Block() as blk:
            @blk.tensor
            def _(tn):
                tn.wait_ge(mV["h"], kvaug_V)
                for t in range(NT):
                    if t >= 2:
                        tn.wait_ge(mV["h"], sC1_v0 + (t - 1))
                    pss = outps(t)
                    mm = None
                    for p in range(NPAIR):
                        mm = tn.matmul(pss[p],
                                       PHQ[:, p, :].rearrange("q (tt c) -> q tt c", tt=NT, c=128)[:, t, :],
                                       KVAUG[:, p, :], start=True, stop=True)
                    bump(mT, mm)

            @blk.vector
            def _(v):
                for t in range(NT):
                    v.wait_ge(mT["h"], sC1_t0 + t + 1)
                    pss = outps(t)
                    if t % 2 == 0:
                        nv1 = psA.ap().rearrange("q (a c) -> q a c", a=8, c=256)[:, :, 64]
                        nv2 = psA.ap().rearrange("q (a c) -> q a c", a=8, c=256)[:, :, 129]
                        v.tensor_scalar(SCC[:, 0:8], nv1, STAB, None, AL.add)
                        v.tensor_scalar(SCC[:, 8:16], nv2, STAB, None, AL.add)
                        v.drain()
                    else:
                        nva = psPK.ap().rearrange("q (a c) -> q a c", a=4, c=256)
                        nvb = psV.ap().rearrange("q (a c) -> q a c", a=4, c=256)
                        v.tensor_scalar(SCC[:, 0:4], nva[:, :, 64], STAB, None, AL.add)
                        v.tensor_scalar(SCC[:, 4:8], nvb[:, :, 64], STAB, None, AL.add)
                        v.tensor_scalar(SCC[:, 8:12], nva[:, :, 129], STAB, None, AL.add)
                        v.tensor_scalar(SCC[:, 12:16], nvb[:, :, 129], STAB, None, AL.add)
                        v.drain()
                    v.reciprocal(SCC[:, 0:16], SCC[:, 0:16])
                    v.drain()
                    for p in range(NPAIR):
                        if t % 2 == 0:
                            sa, sb = p, 8 + p
                        else:
                            sa = p if p < 4 else 4 + (p - 4)
                            sb = (8 + p) if p < 4 else 12 + (p - 4)
                        v.tensor_scalar(TMA[:, p, t, 0:64], pss[p][:, 0:64], SCC[:, sa:sa + 1], None, AL.mult)
                        v.tensor_scalar(TMA[:, p, t, 64:128], pss[p][:, 65:129], SCC[:, sb:sb + 1], None, AL.mult)
                    v.drain()
                    bump(mV, v.drain())
        attnpre_V = mV["n"]

        with nc.Block() as blk:
            @blk.gpsimd
            def _(g):
                g.wait_ge(mS["h"], phiq_S)
                for kc in range(DC):
                    dma(g, qGM, WOS[:, kc, :], wout.ap()[128 * kc:128 * (kc + 1), :])
                g.wait_ge(mV["h"], kvaug_V)
                dma(g, qGM, BROW, bout.ap().rearrange("(x d) -> x d", x=1))
            wo_ld = qGM["n"]

            @blk.scalar
            def _(sc):
                sc.wait_ge(mV["h"], attnpre_V)
                for c in range(DC):
                    o = FMA[:, c, :].rearrange("f (t p) -> f t p", t=NT, p=128)
                    i = TMA[:, c, :, :]
                    dma(sc, qTR[c % 2], o, i, transpose=True)
            tra = (qTR[0]["n"], qTR[1]["n"])

        sC2_t0 = mT["n"]
        sC2_v0 = mV["n"]
        sC2_s0 = mS["n"]

        def c2v(t, j):
            return sC2_v0 + 1 + 3 * t + j

        with nc.Block() as blk:
            @blk.tensor
            def _(tn):
                tn.wait_ge(qGM["h"], wo_ld)
                tn.wait_ge(mV["h"], attnpre_V)
                tn.matmul(psPK[:, 0:512], ONER[:, :], BROW[:, 0:512], start=True, stop=True)
                mm = tn.matmul(psPK[:, 512:1024], ONER[:, :], BROW[:, 512:1024], start=True, stop=True)
                bump(mT, mm)
                tn.wait_ge(qTR[0]["h"], tra[0])
                tn.wait_ge(qTR[1]["h"], tra[1])
                for t in range(NT):
                    ps = psV if (t % 2 == 0) else psPK
                    if t == 1:
                        tn.wait_ge(mV["h"], sC2_v0 + 1)
                    if t >= 2:
                        tn.wait_ge(mV["h"], c2v(t - 2, 1))
                    mm = None
                    for kc in range(DC):
                        lhsT = FMA[:, kc, :].rearrange("f (t p) -> f t p", t=NT, p=128)[:, t, :]
                        for h2 in range(2):
                            mm = tn.matmul(ps[:, 512 * h2:512 * (h2 + 1)], lhsT,
                                           WOS[:, kc, 512 * h2:512 * (h2 + 1)],
                                           start=(kc == 0), stop=(kc == DC - 1))
                    bump(mT, mm)

            xld_cnt = {}
            spill_cnt = {}

            @blk.sync
            def _(sy):
                def spill(j):
                    dma(sy, qSP[j % 2], x2d[128 * j:128 * (j + 1), :], XNEW[:, j % 2, :])
                    dma(sy, qSP[j % 2], hb[128 * j:128 * (j + 1), :].rearrange("p (c f) -> p c f", c=DC, f=128), TMA[:, :, j, :])
                    spill_cnt[j] = qSP[j % 2]["n"]
                for t in range(NT):
                    if t >= 2:
                        sy.wait_ge(mV["h"], c2v(t - 2, 3))
                    xld_cnt[t] = dma(sy, qXL[t % 2], XSTR[:, t % 2, :], xin[128 * t:128 * (t + 1), :])
                    if t >= 2:
                        spill(t - 2)
                for j in (NT - 2, NT - 1):
                    sy.wait_ge(mV["h"], c2v(j, 3))
                    spill(j)

            @blk.vector
            def _(v):
                v.wait_ge(mT["h"], sC2_t0 + 1)
                v.tensor_copy(BCB[:, 0, :], psPK[:, :])
                v.drain()
                bump(mV, v.drain())
                for t in range(NT):
                    ps = psV if (t % 2 == 0) else psPK
                    v.wait_ge(mT["h"], sC2_t0 + 1 + t + 1)
                    v.wait_ge(qXL[t % 2]["h"], xld_cnt[t])
                    if t >= 2:
                        v.wait_ge(mS["h"], sC2_s0 + (t - 2) + 1)
                        v.wait_ge(qSP[t % 2]["h"], spill_cnt[t - 2])
                    v.tensor_add(XNEW[:, t % 2, :], ps[:, :], BCB[:, 0, :])
                    v.drain()
                    v.tensor_add(XNEW[:, t % 2, :], XNEW[:, t % 2, :], XSTR[:, t % 2, :])
                    v.drain()
                    v.bn_stats(STATS[:, 0:6], XNEW[:, t % 2, 0:512])
                    v.bn_stats(STATS[:, 6:12], XNEW[:, t % 2, 512:1024])
                    v.drain()
                    v.bn_aggr(MRS[:, t, 0:2], STATS[:, :])
                    v.drain()
                    v.tensor_scalar(MRS[:, t, 2:3], MRS[:, t, 1:2], 1.0, EPS_LN, AL.mult, AL.add)
                    v.drain()
                    v.reciprocal(MRS[:, t, 3:4], MRS[:, t, 2:3])
                    v.drain()
                    bump(mV, v.drain())
                    v.wait_ge(mS["h"], sC2_s0 + t + 1)
                    v.tensor_scalar(XSTR[:, t % 2, :], XNEW[:, t % 2, :], MRS[:, t, 2:3], MRS[:, t, 3:4], AL.mult, AL.add)
                    v.drain()
                    bump(mV, v.drain())
                    v.tensor_tensor(HLOT[:, :, t, :], XSTR[:, t % 2, :], TMA[:, :, t, :], AL.subtract)
                    v.drain()
                    bump(mV, v.drain())

            @blk.scalar
            def _(sc):
                sc.wait_ge(qTR[0]["h"], tra[0])
                sc.wait_ge(qTR[1]["h"], tra[1])
                for t in range(NT):
                    sc.wait_ge(mV["h"], c2v(t, 1))
                    sc.activation(MRS[:, t, 2:3], MRS[:, t, 3:4], ACT.Sqrt)
                    sc.drain()
                    sc.mul(MRS[:, t, 3:4], MRS[:, t, 0:1], MRS[:, t, 2:3])
                    sc.drain()
                    sc.mul(MRS[:, t, 3:4], MRS[:, t, 3:4], -1.0)
                    sc.drain()
                    sc.activation(TMA[:, :, t, :], XNEW[:, t % 2, :], ACT.Identity,
                                  bias=MRS[:, t, 3:4], scale=MRS[:, t, 2:3])
                    sc.drain()
                    bump(mS, sc.drain())
        hspill = (qSP[0]["n"], qSP[1]["n"])
        hln_S = mS["n"]
        hlo_V = mV["n"]

        if _STAGE == "attn":
            with nc.Block() as blk:
                @blk.sync
                def _(sy):
                    sy.wait_ge(qSP[0]["h"], hspill[0])
                    sy.wait_ge(qSP[1]["h"], hspill[1])
                    for t in range(NT):
                        dma(sy, qST[t % 2], out[128 * t:128 * (t + 1), :], x2d[128 * t:128 * (t + 1), :])
                    sy.wait_ge(qST[0]["h"], qST[0]["n"])
                    sy.wait_ge(qST[1]["h"], qST[1]["n"])
            return nc

        with nc.Block() as blk:
            @blk.gpsimd
            def _(g):
                g.wait_ge(qSP[0]["h"], hspill[0])
                g.wait_ge(qSP[1]["h"], hspill[1])
                g.collective_compute(
                    "AllGather", AL.bypass,
                    replica_groups=[list(range(NCORES))],
                    ins=[hb.ap().opt()], outs=[hall.ap().opt()],
                ).then_inc(sCC["h"], 1)
                sCC["n"] += 1
            hall_cc = sCC["n"]

            @blk.scalar
            def _(sc):
                sc.wait_ge(mS["h"], hln_S)
                sc.wait_ge(qSP[0]["h"], hspill[0])
                sc.wait_ge(qSP[1]["h"], hspill[1])
                for c in range(DC):
                    o = FMA[:, c, :].rearrange("f (t p) -> f t p", t=NT, p=128)
                    i = TMA[:, c, :, :]
                    dma(sc, qTR[c % 2], o, i, transpose=True)
            trh = (qTR[0]["n"], qTR[1]["n"])

            @blk.scalar
            def _(sc):
                sc.wait_ge(mV["h"], hlo_V)
                for c in range(DC):
                    o = HLOF[:, c, :].rearrange("f (t p) -> f t p", t=NT, p=128)
                    i = HLOT[:, c, :, :]
                    dma(sc, qTR[c % 2], o, i, transpose=True)
            trh2 = (qTR[0]["n"], qTR[1]["n"])

        sE_t0 = mT["n"]
        sE_v0 = mV["n"]
        sE_s0 = mS["n"]
        with nc.Block() as blk:
            @blk.tensor
            def _(tn):
                tn.wait_ge(qTR[0]["h"], trh2[0])
                tn.wait_ge(qTR[1]["h"], trh2[1])
                tn.wait_ge(mV["h"], rwc_V)
                first = True
                for ph in range(3):
                    for kc in range(DC):
                        lhsT = [RWHI[:, kc, :], RWLO[:, kc, :], RWHI[:, kc, :]][ph]
                        rhs = [FMA, FMA, HLOF][ph][:, kc, :]
                        last = (ph == 2 and kc == DC - 1)
                        for h4 in range(4):
                            mm = tn.matmul(psA[0:8, 512 * h4:512 * (h4 + 1)], lhsT,
                                           rhs[:, 512 * h4:512 * (h4 + 1)],
                                           start=first, stop=last)
                        first = False
                bump(mT, mm)
            lg_T = mT["n"]

            @blk.scalar
            def _(sc):
                sc.wait_ge(mT["h"], lg_T)
                sc.activation(LGT[0:8, 0, :], psA[0:8, :], ACT.Identity, bias=RWBS[:, 0:1])
                sc.drain()
                bump(mS, sc.drain())
            lg_S = mS["n"]

            @blk.tensor
            def _(tn):
                tn.wait_ge(mS["h"], lg_S)
                for t in range(NT):
                    ps = psPK if (t % 2 == 0) else psV
                    if t >= 2:
                        tn.wait_ge(mV["h"], sE_v0 + (t - 1))
                    bump(mT, tn.transpose(ps[0:128, 0:8], LGT[0:8, 0, 128 * t:128 * (t + 1)], ID8S[:, :]))

            @blk.vector
            def _(v):
                for t in range(NT):
                    ps = psPK if (t % 2 == 0) else psV
                    v.wait_ge(mT["h"], lg_T + t + 1)
                    v.tensor_copy(LTM[:, t, :], ps[0:128, 0:8])
                    v.drain()
                    bump(mV, v.drain())
            ltm_V = mV["n"]

            @blk.vector
            def _(v):
                for t in range(NT):
                    v.max(out=SC8[:, :], in_=LTM[:, t, :])
                    v.drain()
                    v.tensor_copy(M12[:, t, 0:2], SC8[:, 0:2])
                    v.drain()
                    v.tensor_scalar(RTS[:, t, 0:1], SC8[:, 0:1], -1.0, None, AL.mult)
                    v.drain()
                    v.tensor_scalar(MSK[:, t, :], LTM[:, t, :], SC8[:, 1:2], None, AL.is_ge)
                    v.drain()
                    bump(mV, v.drain())
            msk_V = mV["n"]

            @blk.scalar
            def _(sc):
                for t in range(NT):
                    sc.wait_ge(mV["h"], ltm_V + t + 1)
                    sc.activation(E8T[:, t, :], LTM[:, t, :], ACT.Exp, bias=RTS[:, t, 0:1])
                    sc.drain()
                    bump(mS, sc.drain())

            @blk.vector
            def _(v):
                for t in range(NT):
                    v.wait_ge(mS["h"], lg_S + t + 1)
                    v.reduce_sum(RTS[:, t, 1:2], E8T[:, t, :], axis=AXX)
                    v.drain()
                    v.reciprocal(RTS[:, t, 2:3], RTS[:, t, 1:2])
                    v.drain()
                    v.tensor_scalar(GTS[:, t, :], E8T[:, t, :], RTS[:, t, 2:3], None, AL.mult)
                    v.drain()
                    bump(mV, v.drain())
            gts_V = mV["n"]

            @blk.tensor
            def _(tn):
                tn.wait_ge(mV["h"], msk_V)
                mm = None
                for t in range(NT):
                    mm = tn.matmul(psA[0:8, 2000 + t:2001 + t], MSK[:, t, :], ONEC[:, :], start=True, stop=True)
                bump(mT, mm)
            cntT_T = mT["n"]

            @blk.vector
            def _(v):
                v.wait_ge(mT["h"], cntT_T)
                v.tensor_copy(CNTT[:, :], psA[0:8, 2000:2016])
                v.drain()
                bump(mV, v.drain())
            cntT_V = mV["n"]

            @blk.tensor
            def _(tn):
                tn.wait_ge(mV["h"], cntT_V)
                tn.wait_ge(mV["h"], msk_V)
                bump(mT, tn.transpose(psV[0:16, 0:8], CNTT[:, :], ID8S[:, :]))
            cnts_T = mT["n"]

            @blk.vector
            def _(v):
                v.wait_ge(mT["h"], cnts_T)
                v.tensor_copy(CNTS[:, :], psV[0:16, 0:8])
                v.drain()
                bump(mV, v.drain())
            cnts_V = mV["n"]

            ecnt = {}

            @blk.sync
            def _(sy):
                sy.wait_ge(mV["h"], cnts_V)
                ecnt["cnt_st"] = dma(sy, qMS, cnt_i[:, :], CNTS[:, :])
                dma(sy, qMS, TSEL, tsel[:, :])
                dma(sy, qMS, TRIS, tri[:, :])
            tsel_ld = qMS["n"]
            cnt_st = ecnt["cnt_st"]

            @blk.gpsimd
            def _(g):
                g.wait_ge(qMS["h"], cnt_st)
                g.collective_compute(
                    "AllGather", AL.bypass,
                    replica_groups=[list(range(NCORES))],
                    ins=[cnt_i.ap().opt()], outs=[cnt_o.ap().opt()],
                ).then_inc(sCC["h"], 1)
                sCC["n"] += 1
            cnt_cc = sCC["n"]

            @blk.sync
            def _(sy):
                sy.wait_ge(sCC["h"], cnt_cc)
                dma(sy, qMS, CAG[:, :], cnt_o[:, :])
            cag_ld = qMS["n"]

            @blk.tensor
            def _(tn):
                tn.wait_ge(qMS["h"], cag_ld)
                tn.wait_ge(qMS["h"], tsel_ld)
                bump(mT, tn.matmul(psPK[:, 0:8], TRIS, CAG[:, :], start=True, stop=True))
            gb_T = mT["n"]

            @blk.vector
            def _(v):
                v.wait_ge(mT["h"], gb_T)
                v.tensor_copy(GBS[:, :], psPK[:, 0:8])
                v.drain()
                bump(mV, v.drain())
            gb_V = mV["n"]

            @blk.tensor
            def _(tn):
                tn.wait_ge(mV["h"], gb_V)
                bump(mT, tn.matmul(psV[0:16, 0:8], SELS[:, :], GBS[:, :], start=True, stop=True))
            mb_T = mT["n"]

            @blk.vector
            def _(v):
                v.wait_ge(mT["h"], mb_T)
                v.tensor_copy(MBS[:, :], psV[0:16, 0:8])
                v.drain()
                bump(mV, v.drain())
            mb_V = mV["n"]

            @blk.tensor
            def _(tn):
                tn.wait_ge(mV["h"], mb_V)
                tn.wait_ge(qMS["h"], tsel_ld)
                for t in range(NT):
                    ps = psPK if (t % 2 == 0) else psV
                    if t >= 2:
                        tn.wait_ge(mV["h"], mb_V + (t - 1))
                    tn.matmul(ps[:, 0:8], TRIS, MSK[:, t, :], start=True, stop=False)
                    bump(mT, tn.matmul(ps[:, 0:8], TSEL[:, 128 * t:128 * (t + 1)], MBS[:, :], start=False, stop=True))

            @blk.vector
            def _(v):
                for t in range(NT):
                    ps = psPK if (t % 2 == 0) else psV
                    v.wait_ge(mT["h"], mb_T + t + 1)
                    v.tensor_copy(POS[:, t, :], ps[:, 0:8])
                    v.drain()
                    bump(mV, v.drain())
            pos_V = mV["n"]

            @blk.vector
            def _(v):
                for t in range(NT):
                    v.tensor_scalar(SC8[:, :], POS[:, t, :], float(CAP), None, AL.is_lt)
                    v.drain()
                    v.tensor_tensor(MSK[:, t, :], MSK[:, t, :], SC8[:, :], AL.mult)
                    v.drain()
                    v.tensor_tensor(WGT[:, t, :], GTS[:, t, :], MSK[:, t, :], AL.mult)
                    v.drain()
                    for k in range(2):
                        oi = OI1 if k == 0 else OI2
                        v.tensor_scalar(SC8[:, :], LTM[:, t, :], M12[:, t, k:k + 1], None, AL.is_equal)
                        v.drain()
                        v.tensor_tensor(SC8B[:, :], SC8[:, :], IOTS[:, :], AL.mult)
                        v.drain()
                        v.reduce_sum(EPW[:, 2 * k, t:t + 1], SC8B[:, :], axis=AXX)
                        v.drain()
                        v.tensor_tensor(SC8B[:, :], SC8[:, :], POS[:, t, :], AL.mult)
                        v.drain()
                        v.reduce_sum(EPW[:, 2 * k + 1, t:t + 1], SC8B[:, :], axis=AXX)
                        v.drain()
                        v.tensor_tensor(SC8B[:, :], SC8[:, :], WGT[:, t, :], AL.mult)
                        v.drain()
                        v.reduce_sum(WCOL[:, k, t:t + 1], SC8B[:, :], axis=AXX)
                        v.drain()
                        v.tensor_scalar(SCC[:, 0:1], EPW[:, 2 * k + 1, t:t + 1], float(CAP + 2 * t + k), None, AL.min)
                        v.drain()
                        v.tensor_copy(OIS[:, 0:1], SCC[:, 0:1])
                        v.drain()
                        v.tensor_scalar(OIS[:, 1:2], OIS[:, 0:1], 4, None, AL.arith_shift_right)
                        v.tensor_scalar(OIS[:, 2:3], OIS[:, 0:1], 15, None, AL.bitwise_and)
                        v.drain()
                        v.tensor_scalar(OIS[:, 2:3], OIS[:, 2:3], ILC, None, AL.mult)
                        v.tensor_copy(OIS[:, 3:4], EPW[:, 2 * k, t:t + 1])
                        v.drain()
                        v.tensor_scalar(OIS[:, 3:4], OIS[:, 3:4], ESTR, None, AL.mult)
                        v.drain()
                        v.tensor_tensor(OIS[:, 3:4], OIS[:, 3:4], OIS[:, 2:3], AL.add)
                        v.drain()
                        v.tensor_tensor(oi[:, t:t + 1], OIS[:, 3:4], OIS[:, 1:2], AL.add)
                        v.drain()
                        v.tensor_scalar(SCC[:, 4:5], EPW[:, 2 * k + 1, t:t + 1], float(CAP - 1), None, AL.min)
                        v.drain()
                        v.tensor_copy(OIS[:, 0:1], SCC[:, 4:5])
                        v.drain()
                        v.tensor_scalar(OIS[:, 1:2], OIS[:, 0:1], 9, None, AL.arith_shift_right)
                        v.tensor_copy(OIS[:, 2:3], EPW[:, 2 * k, t:t + 1])
                        v.drain()
                        v.tensor_scalar(OIS[:, 1:2], OIS[:, 1:2], 3584, None, AL.mult)
                        v.tensor_scalar(OIS[:, 2:3], OIS[:, 2:3], 512, None, AL.mult)
                        v.drain()
                        v.tensor_tensor(OIS[:, 0:1], OIS[:, 0:1], OIS[:, 1:2], AL.add)
                        v.drain()
                        v.tensor_tensor(OIS[:, 0:1], OIS[:, 0:1], OIS[:, 2:3], AL.add)
                        v.drain()
                        v.tensor_copy(YOF[:, k, t:t + 1], OIS[:, 0:1])
                        v.drain()
                    bump(mV, v.drain())
            offs_V = mV["n"]

            @blk.gpsimd
            def _(g):
                g.memset(ZTAB, 0.0)
                g.drain()
                dma(g, qSC, tab_i.ap().rearrange("(p c) x -> p (c x)", p=128), ZTAB)
                g.wait_ge(qSC["h"], qSC["n"])
                for t in range(NT):
                    g.wait_ge(mV["h"], pos_V + t + 1)
                    for oi in (OI1, OI2):
                        g.wait_ge(qSC["h"], qSC["n"])
                        g.indirect_dma_start(
                            out=tab_i.ap(), out_offset=IndirectOffsetOnAxis(ap=oi[:, t:t + 1], axis=0),
                            in_=TOKF[:, t:t + 1], in_offset=None,
                        ).then_inc(qSC["h"], 16)
                        qSC["n"] += 16
                g.wait_ge(qSC["h"], qSC["n"])
                g.collective_compute(
                    "ReduceScatter", AL.add,
                    replica_groups=[list(range(NCORES))],
                    ins=[tab_i.ap().opt()], outs=[tab_o.ap().opt()],
                ).then_inc(sCC["h"], 1)
                sCC["n"] += 1
            tab_cc = sCC["n"]

            @blk.sync
            def _(sy):
                sy.wait_ge(mV["h"], offs_V)
                dma(sy, qMS, yd1.ap().rearrange("(p t) -> p t", p=128), YOF[:, 0, :])
                dma(sy, qMS, yd2.ap().rearrange("(p t) -> p t", p=128), YOF[:, 1, :])

            @blk.sync
            def _(sy):
                src1 = bass.AP(yd1, 0, [[16, 16], [1, 16], [256, 8]])
                src2 = bass.AP(yd2, 0, [[16, 16], [1, 16], [256, 8]])
                dma(sy, qMS, YIFA.rearrange("q (t ph) -> q t ph", t=16, ph=8), src1, allow_slow_non_contiguous=True)
                dma(sy, qMS, YIFB.rearrange("q (t ph) -> q t ph", t=16, ph=8), src2, allow_slow_non_contiguous=True)
            yif_ld = qMS["n"]

            @blk.vector
            def _(v):
                v.wait_ge(qMS["h"], yif_ld)
                v.memset(YI16A[:, :], 0)
                v.memset(YI16B[:, :], 0)
                v.drain()
                v.tensor_copy(YI16A[0:16, :], YIFA)
                v.tensor_copy(YI16B[0:16, :], YIFB)
                v.drain()
                bump(mV, v.drain())
            yi16_V = mV["n"]

            @blk.sync
            def _(sy):
                sy.wait_ge(mV["h"], yi16_V)
                dma(sy, qMS, YI16A[16:32, :], YI16A[0:16, :])
                dma(sy, qMS, YI16B[16:32, :], YI16B[0:16, :])
            yi16_dup = qMS["n"]

            @blk.sync
            def _(sy):
                sy.wait_ge(sCC["h"], tab_cc)
                dma(sy, qMS, XIF, tab_o.ap().rearrange("(q c) x -> q (c x)", q=16))
            xif_ld = qMS["n"]

            @blk.vector
            def _(v):
                v.wait_ge(qMS["h"], xif_ld)
                v.memset(XI16[:, :], 0)
                v.drain()
                v.tensor_copy(XI16[0:16, :], XIF)
                v.drain()
                bump(mV, v.drain())
            xi16_Vl = mV["n"]

            @blk.sync
            def _(sy):
                sy.wait_ge(mV["h"], xi16_Vl)
                dma(sy, qMS, XI16[16:32, :], XI16[0:16, :])
            xi16_dup = qMS["n"]
        xi16_V = mV["n"]

        if _STAGE == "router":
            fmadbg = nc.dram_tensor("fmadbg", [128, DC, 2048], BF16, kind="ExternalOutput")
            hlodbg = nc.dram_tensor("hlodbg", [128, DC, 2048], BF16, kind="ExternalOutput")
            lgtdbg = nc.dram_tensor("lgtdbg", [8, 2048], F32, kind="ExternalOutput")
            ltmdbg2 = nc.dram_tensor("ltmdbg2", [128, NT, NEXP], F32, kind="ExternalOutput")
            with nc.Block() as blk:
                @blk.sync
                def _(sy):
                    sy.wait_ge(mV["h"], offs_V)
                    dma(sy, qMS, fmadbg[:, :, :], FMA[:, :, :])
                    dma(sy, qMS, hlodbg[:, :, :], HLOF[:, :, :])
                    dma(sy, qMS, lgtdbg[:, :], LGT[0:8, 0, :])
                    dma(sy, qMS, ltmdbg2[:, :, :], LTM[:, :, :])
                    sy.wait_ge(qMS["h"], qMS["n"])
            return nc

        sF_v0 = mV["n"]
        sF_s0 = mS["n"]
        with nc.Block() as blk:
            @blk.gpsimd
            def _(g):
                dma(g, qGM, BROW, b2e.ap().rearrange("(x d) -> x d", x=1))
            b2_ld = qGM["n"]

            @blk.tensor
            def _(tn):
                tn.wait_ge(qGM["h"], b2_ld)
                tn.wait_ge(mV["h"], pos_V)
                tn.matmul(psPK[:, 0:512], ONER[:, :], BROW[:, 0:512], start=True, stop=True)
                bump(mT, tn.matmul(psPK[:, 512:1024], ONER[:, :], BROW[:, 512:1024], start=True, stop=True))
            b2b_T = mT["n"]

            @blk.vector
            def _(v):
                v.wait_ge(mT["h"], b2b_T)
                v.tensor_copy(BCB[:, 0, :], psPK[:, :])
                v.drain()
                bump(mV, v.drain())
            b2b_V = mV["n"]

        F_t0 = mT["n"]

        def f_mm1(c, m):
            return F_t0 + 2 + 66 * c + m + 1

        def f_xtrA(c):
            return F_t0 + 1 if c == 0 else F_t0 + 2 + 66 * (c - 1) + 33

        def f_xtrB(c):
            return F_t0 + 2 if c == 0 else F_t0 + 2 + 66 * (c - 1) + 34

        def f_mm2(c, m):
            return F_t0 + 2 + 66 * c + 34 + m + 1

        F_s0 = mS["n"]

        def f_gelu(c, m):
            return F_s0 + 32 * c + m + 1

        F_v0 = mV["n"]

        def f_xtrAv(c):
            return F_v0 + 1 if c == 0 else F_v0 + 2 + 6 * (c - 1) + 1

        def f_xtrBv(c):
            return F_v0 + 2 if c == 0 else F_v0 + 2 + 6 * (c - 1) + 2

        def f_eout(c, rt):
            return F_v0 + 2 + 6 * c + 2 + rt + 1

        F_mg0 = mG["n"]
        xg_c = {}
        w1_c = {}

        def emit_xtr_mms(tn, c, batch):
            mm = None
            for j in range(16):
                rb, fq = j // 4, j % 4
                fc = 4 * batch + fq
                mm = tn.matmul(psA[:, 128 * j:128 * (j + 1)],
                               XTM[:, rb, 128 * fc:128 * (fc + 1)], IDB[:, :],
                               start=True, stop=True)
            bump(mT, mm)

        def emit_xtr_evac(v, c, batch):
            dstv = XG[:, 4 * batch:4 * batch + 4, :].rearrange("p fq (rb q) -> p fq rb q", rb=4, q=128)
            srcv = bass.AP(psA, 0, [[2048, 128], [128, 4], [512, 4], [1, 128]])
            v.tensor_copy(dstv, srcv)
            v.drain()
            bump(mV, v.drain())

        with nc.Block() as blk:
            @blk.gpsimd
            def _(g):
                g.wait_ge(sCC["h"], hall_cc)
                g.wait_ge(mV["h"], xi16_V)
                g.wait_ge(qMS["h"], xi16_dup)
                for c in range(NCH):
                    if c >= 1:
                        g.wait_ge(mT["h"], f_xtrB(c - 1))
                    qq = qGA[c % 2]
                    if qq["n"] > 0:
                        g.wait_ge(qq["h"], qq["n"])
                    g.dma_gather(XTM[:, :, :], hall.ap(), XI16[:, 32 * c:32 * (c + 1)],
                                 num_idxs=RCH, num_idxs_reg=RCH, elem_size=D).then_inc(qq["h"], 16)
                    qq["n"] += 16
                    xg_c[c] = qq["n"]
                    bump(mG, g.drain())
                    if c == 0:
                        g.wait_ge(qPC[0]["h"], wpre[0])
                        g.wait_ge(qPC[1]["h"], wpre[1])
                    for m in range(MHID):
                        if 32 * c + m >= 4:
                            pc, pm = (c, m - 4) if m >= 4 else (c - 1, m + 28)
                            g.wait_ge(mT["h"], f_mm1(pc, pm))
                        w1_c[(c, m)] = dma(g, qWQ[m % 2], W1S[:, m % 4, :, :],
                                           w1bf[128 * m:128 * (m + 1), :].rearrange("q (kc mm) -> q kc mm", kc=DC, mm=128))
            w2cnt = {}
            estcnt = {}

            @blk.sync
            def _(sy):
                def store(cc, rt):
                    n = 4 * cc + rt
                    sy.wait_ge(mV["h"], f_eout(cc, rt))
                    estcnt[n] = dma(sy, qST[n % 2], eout_b[512 * cc + 128 * rt:512 * cc + 128 * (rt + 1), :],
                                    EOB[:, rt % 2, :])
                sy.wait_ge(qPC[0]["h"], wpre[0])
                sy.wait_ge(qPC[1]["h"], wpre[1])
                for c in range(NCH):
                    for m in range(MHID):
                        if 32 * c + m >= 4:
                            pc, pm = (c, m - 4) if m >= 4 else (c - 1, m + 28)
                            sy.wait_ge(mT["h"], f_mm2(pc, pm))
                        w2cnt[(c, m)] = dma(sy, qW2[m % 2], W2S[:, m % 4, :], w2bf[128 * m:128 * (m + 1), :])
                        if c >= 1 and m < 4:
                            store(c - 1, m)
                for rt in range(4):
                    store(NCH - 1, rt)
                sy.wait_ge(qST[0]["h"], qST[0]["n"])
                sy.wait_ge(qST[1]["h"], qST[1]["n"])

            @blk.tensor
            def _(tn):
                tn.wait_ge(mV["h"], b2b_V)
                tn.wait_ge(mG["h"], F_mg0 + 1)
                tn.wait_ge(qGA[0]["h"], xg_c[0])
                emit_xtr_mms(tn, 0, 0)
                tn.wait_ge(mV["h"], f_xtrAv(0))
                emit_xtr_mms(tn, 0, 1)
                for c in range(NCH):
                    for m in range(MHID):
                        ps = [psPK[:, 0:512], psPK[:, 512:1024], psV[:, 0:512], psV[:, 512:1024]][m % 4]
                        tn.wait_ge(qWQ[m % 2]["h"], w1_c[(c, m)])
                        if m == 0:
                            tn.wait_ge(mV["h"], f_xtrBv(c))
                        gi = 32 * c + m
                        if gi >= 4:
                            pc, pm = (c, m - 4) if m >= 4 else (c - 1, m + 28)
                            tn.wait_ge(mS["h"], f_gelu(pc, pm))
                        if m == 0 and c > 0:
                            tn.wait_ge(mV["h"], f_eout(c - 1, 3))
                        mm = None
                        for kc in range(DC):
                            mm = tn.matmul(ps, W1S[:, m % 4, kc, :], XG[:, kc, :],
                                           start=(kc == 0), stop=(kc == DC - 1))
                        bump(mT, mm)
                    if c + 1 < NCH:
                        tn.wait_ge(mG["h"], F_mg0 + c + 2)
                        tn.wait_ge(qGA[(c + 1) % 2]["h"], xg_c[c + 1])
                        if c >= 1:
                            tn.wait_ge(mV["h"], f_eout(c - 1, 1))
                        emit_xtr_mms(tn, c + 1, 0)
                        tn.wait_ge(mV["h"], f_xtrAv(c + 1))
                        emit_xtr_mms(tn, c + 1, 1)
                    else:
                        bump(mT, tn.drain())
                        bump(mT, tn.drain())
                    for m in range(MHID):
                        tn.wait_ge(qW2[m % 2]["h"], w2cnt[(c, m)])
                        if m == 0:
                            tn.wait_ge(mS["h"], F_s0 + 32 * (c + 1))
                            tn.wait_ge(mV["h"], f_xtrBv(c + 1) if c + 1 < NCH else mV["n"])
                            if c > 0:
                                tn.wait_ge(mV["h"], f_eout(c - 1, 3))
                        mm = None
                        for rt in range(4):
                            pst = [psA, psA, psPK, psV][rt]
                            base = [0, 1024, 0, 0][rt]
                            lhsT = H1S[:, m, :].rearrange("q (rt x) -> q rt x", rt=4, x=128)[:, rt, :]
                            for h2 in range(2):
                                mm = tn.matmul(pst[:, base + 512 * h2:base + 512 * (h2 + 1)],
                                               lhsT, W2S[:, m % 4, 512 * h2:512 * (h2 + 1)],
                                               start=(m == 0), stop=(m == MHID - 1))
                        bump(mT, mm)

            @blk.scalar
            def _(sc):
                for c in range(NCH):
                    for m in range(MHID):
                        sc.wait_ge(mT["h"], f_mm1(c, m))
                        if c > 0:
                            sc.wait_ge(mT["h"], f_mm2(c - 1, m))
                        ps = [psPK[:, 0:512], psPK[:, 512:1024], psV[:, 0:512], psV[:, 512:1024]][m % 4]
                        bump(mS, sc.activation(H1S[:, m, :], ps, ACT.Gelu, bias=B1S[:, m:m + 1]))

            @blk.vector
            def _(v):
                v.wait_ge(mT["h"], f_xtrA(0))
                emit_xtr_evac(v, 0, 0)
                v.wait_ge(mT["h"], f_xtrB(0))
                emit_xtr_evac(v, 0, 1)
                for c in range(NCH):
                    if c + 1 < NCH:
                        v.wait_ge(mT["h"], f_xtrA(c + 1))
                        emit_xtr_evac(v, c + 1, 0)
                        v.wait_ge(mT["h"], f_xtrB(c + 1))
                        emit_xtr_evac(v, c + 1, 1)
                    else:
                        bump(mV, v.drain())
                        bump(mV, v.drain())
                    v.wait_ge(mT["h"], f_mm2(c, 31))
                    for rt in range(4):
                        if 4 * c + rt >= 2:
                            v.wait_ge(qST[(4 * c + rt) % 2]["h"], estcnt[4 * c + rt - 2])
                        ps2 = [psA[:, 0:1024], psA[:, 1024:2048], psPK[:, :], psV[:, :]][rt]
                        bump(mV, v.tensor_add(EOB[:, rt % 2, :], ps2, BCB[:, 0, :]))
        f_eout_done = (qST[0]["n"], qST[1]["n"])

        if _STAGE == "eout":
            eodbg = nc.dram_tensor("eodbg", [CAP, D], BF16, kind="ExternalOutput")
            xgdbg = nc.dram_tensor("xgdbg", [128, 8192], BF16, kind="ExternalOutput")
            h1dbg = nc.dram_tensor("h1dbg", [128, MHID, RCH], BF16, kind="ExternalOutput")
            halldbg = nc.dram_tensor("halldbg", [NCORES * TCORE, D], BF16, kind="ExternalOutput")
            tabdbg = nc.dram_tensor("tabdbg", [ESTR], F32, kind="ExternalOutput")
            tabidbg = nc.dram_tensor("tabidbg", [NEXP * ESTR], F32, kind="ExternalOutput")
            epwdbg = nc.dram_tensor("epwdbg", [128, 4, NT], F32, kind="ExternalOutput")
            oi1dbg = nc.dram_tensor("oi1dbg", [128, NT], I32, kind="ExternalOutput")
            oi2dbg = nc.dram_tensor("oi2dbg", [128, NT], I32, kind="ExternalOutput")
            cntdbg = nc.dram_tensor("cntdbg", [128, NEXP], F32, kind="ExternalOutput")
            posdbg = nc.dram_tensor("posdbg", [128, NT, NEXP], F32, kind="ExternalOutput")
            mbsdbg = nc.dram_tensor("mbsdbg", [16, NEXP], F32, kind="ExternalOutput")
            mskdbg = nc.dram_tensor("mskdbg", [128, NT, NEXP], F32, kind="ExternalOutput")
            ltmdbg = nc.dram_tensor("ltmdbg", [128, NT, NEXP], F32, kind="ExternalOutput")
            gtsdbg = nc.dram_tensor("gtsdbg", [128, NT, NEXP], F32, kind="ExternalOutput")
            with nc.Block() as blk:
                @blk.sync
                def _(sy):
                    sy.wait_ge(qST[0]["h"], f_eout_done[0])
                    sy.wait_ge(qST[1]["h"], f_eout_done[1])
                    dma(sy, qMS, eodbg[:, :], eout_b[:, :])
                    dma(sy, qMS, xgdbg[:, :], XGS[:, :])
                    dma(sy, qMS, h1dbg[:, :, :], H1S[:, :, :])
                    dma(sy, qMS, halldbg[:, :], hall[:, :])
                    dma(sy, qMS, tabdbg.ap(), tab_o.ap().rearrange("(a) x -> (a x)"))
                    dma(sy, qMS, tabidbg.ap(), tab_i.ap().rearrange("(a) x -> (a x)"))
                    dma(sy, qMS, epwdbg[:, :, :], EPW[:, :, :])
                    dma(sy, qMS, oi1dbg[:, :], OI1[:, :])
                    dma(sy, qMS, oi2dbg[:, :], OI2[:, :])
                    dma(sy, qMS, cntdbg[:, :], cnt_o[:, :])
                    dma(sy, qMS, posdbg[:, :, :], POS[:, :, :])
                    dma(sy, qMS, mbsdbg[:, :], MBS[:, :])
                    dma(sy, qMS, mskdbg[:, :, :], MSK[:, :, :])
                    dma(sy, qMS, ltmdbg[:, :, :], LTM[:, :, :])
                    dma(sy, qMS, gtsdbg[:, :, :], GTS[:, :, :])
                    sy.wait_ge(qMS["h"], qMS["n"])
            return nc

        with nc.Block() as blk:
            @blk.gpsimd
            def _(g):
                for c in range(NCH):
                    g.wait_ge(qST[0]["h"], estcnt[4 * c + 2])
                    g.wait_ge(qST[1]["h"], estcnt[4 * c + 3])
                    g.collective_compute(
                        "AllGather", AL.bypass,
                        replica_groups=[list(range(NCORES))],
                        ins=[eout_b[512 * c:512 * (c + 1), :].opt()],
                        outs=[eout_all[4096 * c:4096 * (c + 1), :].opt()],
                    ).then_inc(sCC["h"], 1)
                    sCC["n"] += 1
            eall_cc = sCC["n"]

        G_v0 = mV["n"]
        G_mg0 = mG["n"]
        with nc.Block() as blk:
            @blk.gpsimd
            def _(g):
                g.wait_ge(sCC["h"], eall_cc)
                g.wait_ge(mV["h"], yi16_V)
                g.wait_ge(qMS["h"], yi16_dup)
                for q in range(4):
                    qq = qGA[0]
                    if qq["n"] > 0:
                        g.wait_ge(qq["h"], qq["n"])
                    g.dma_gather(G1B[:, 4 * q:4 * (q + 1), :], eout_all.ap(), YI16A[:, 32 * q:32 * (q + 1)],
                                 num_idxs=512, num_idxs_reg=512, elem_size=D).then_inc(qq["h"], 16)
                    qq["n"] += 16
                    qq2 = qGA[1]
                    if qq2["n"] > 0:
                        g.wait_ge(qq2["h"], qq2["n"])
                    g.dma_gather(G2B[:, 4 * q:4 * (q + 1), :], eout_all.ap(), YI16B[:, 32 * q:32 * (q + 1)],
                                 num_idxs=512, num_idxs_reg=512, elem_size=D).then_inc(qq2["h"], 16)
                    qq2["n"] += 16
            g1_dma = qGA[0]["n"]
            g2_dma = qGA[1]["n"]

            gld_cnt = {}
            gst_cnt = {}

            @blk.sync
            def _(sy):
                def store(j):
                    sy.wait_ge(mV["h"], G_v0 + j + 1)
                    gst_cnt[j] = dma(sy, qST[j % 2], out[128 * j:128 * (j + 1), :], XNEW[:, j % 2, :])
                for t in range(NT):
                    if t >= 2:
                        sy.wait_ge(mV["h"], G_v0 + (t - 2) + 1)
                    gld_cnt[t] = dma(sy, qXL[t % 2], XSTR[:, t % 2, :], x2d[128 * t:128 * (t + 1), :])
                    if t >= 2:
                        store(t - 2)
                store(NT - 2)
                store(NT - 1)
                sy.wait_ge(qST[0]["h"], qST[0]["n"])
                sy.wait_ge(qST[1]["h"], qST[1]["n"])

            @blk.vector
            def _(v):
                for t in range(NT):
                    v.wait_ge(qGA[0]["h"], g1_dma - 64 + 16 * (t // 4 + 1))
                    v.wait_ge(qGA[1]["h"], g2_dma - 64 + 16 * (t // 4 + 1))
                    v.wait_ge(qXL[t % 2]["h"], gld_cnt[t])
                    if t >= 2:
                        v.wait_ge(qST[t % 2]["h"], gst_cnt[t - 2])
                    v.scalar_tensor_tensor(XNEW[:, t % 2, :], G1B[:, t, :], WCOL[:, 0, t:t + 1],
                                           XSTR[:, t % 2, :], AL.mult, AL.add)
                    v.drain()
                    bump(mV, v.scalar_tensor_tensor(XNEW[:, t % 2, :], G2B[:, t, :], WCOL[:, 1, t:t + 1],
                                                    XNEW[:, t % 2, :], AL.mult, AL.add))

        return nc


def _host_consts(core):
    tri = np.triu(np.ones((128, 128), np.float32), 1)
    selm = np.zeros((128, 16), np.float32)
    for i in range(16):
        selm[16 * core + i, i] = 1.0
    iota8 = np.broadcast_to(np.arange(8, dtype=np.float32), (128, 8)).copy()
    tokid = np.zeros((128, NT), np.float32)
    for t in range(NT):
        for p in range(128):
            tokid[p, t] = core * TCORE + t * 128 + p
    onesr = np.ones((1, 128), np.float32)
    onec = np.ones((128, 1), np.float32)
    id8 = np.eye(8, dtype=np.float32)
    tsel = np.zeros((16, 2048), np.float32)
    for t in range(16):
        tsel[t, 128 * t:128 * (t + 1)] = 1.0
    import ml_dtypes
    idb = np.eye(128, dtype=ml_dtypes.bfloat16)
    return dict(tri=tri, selm=selm, iota8=iota8, tokid=tokid, onesr=onesr,
                onec=onec, id8=id8, tsel=tsel, idb=idb)


def kernel(L0_attn_x, L0_attn_W_qkv, L0_attn_RF, L0_attn_W_out, L0_attn_b_out,
           L0_ffn_router_w, L0_ffn_router_b, L0_ffn_w1, L0_ffn_b1, L0_ffn_w2, L0_ffn_b2):
    x = np.ascontiguousarray(np.asarray(L0_attn_x, dtype=np.float32)).reshape(NCORES * TCORE, D)
    wq0 = np.asarray(L0_attn_W_qkv, dtype=np.float32)
    wqb = wq0[:, 0:D].reshape(DC, 128, NPAIR, 128).transpose(2, 1, 0, 3).reshape(D, D)
    wkb = wq0[:, D:2 * D].reshape(DC, 128, NPAIR, 128).transpose(2, 1, 0, 3).reshape(D, D)
    rww0 = np.asarray(L0_ffn_router_w, dtype=np.float32)
    args = dict(
        wqkv=np.ascontiguousarray(np.stack([wqb, wkb, wq0[:, 2 * D:3 * D]], axis=0)),
        rfin=np.ascontiguousarray(np.asarray(L0_attn_RF, dtype=np.float32)),
        wout=np.ascontiguousarray(np.asarray(L0_attn_W_out, dtype=np.float32)),
        bout=np.ascontiguousarray(np.asarray(L0_attn_b_out, dtype=np.float32)),
        rww=np.ascontiguousarray(rww0.reshape(DC, 128, NEXP).transpose(1, 0, 2).reshape(128, DC * NEXP)),
        rwb=np.ascontiguousarray(np.asarray(L0_ffn_router_b, dtype=np.float32)),
    )
    w1 = np.ascontiguousarray(np.asarray(L0_ffn_w1, dtype=np.float32))
    b1 = np.ascontiguousarray(np.asarray(L0_ffn_b1, dtype=np.float32))
    w2 = np.ascontiguousarray(np.asarray(L0_ffn_w2, dtype=np.float32))
    b2 = np.ascontiguousarray(np.asarray(L0_ffn_b2, dtype=np.float32))

    nc = build_nc()
    mybir.codegen_inst_isa_subclasses(nc)
    in_maps = []
    for c in range(NCORES):
        m = dict(args)
        m["xin"] = x[c * TCORE:(c + 1) * TCORE]
        m["w1e"] = np.ascontiguousarray(
            w1[c].reshape(DC, 128, MHID, 128).transpose(2, 1, 0, 3).reshape(MHID * 128, DC * 128))
        m["b1e"] = np.ascontiguousarray(b1[c].reshape(MHID, 128).T)
        m["w2e"] = w2[c]
        m["b2e"] = b2[c]
        m.update(_host_consts(c))
        in_maps.append(m)
    res = run_bass_kernel_spmd(nc, in_maps, core_ids=list(range(NCORES)))
    y = np.concatenate([np.asarray(r["out"]) for r in res.results], axis=0)
    return y.reshape(2, 8192, 1024)
